# version 20
# baseline (speedup 1.0000x reference)
import numpy as np
import concourse.bass as bass
import concourse.mybir as mybir
from concourse.bass_utils import run_bass_kernel_spmd

F32 = mybir.dt.float32
BF16 = mybir.dt.bfloat16
ALU = mybir.AluOpType
ACT = mybir.ActivationFunctionType
AX = mybir.AxisListType

NCORES = 8
T = 2048
C = 1024
H = 16
K = 31
R = 64
F = H * K
NT = 16
NSUB = 32
EPS = 1e-5

WROW = 16 * 128 + 1
P_SUB = 128 * 1025
WET_BATCH = 4
WET_ROWS_PAD = 4112
N_WB = NSUB // WET_BATCH
BPITCH = 4 * P_SUB + 1536

LVL_FULL = 6


def build_nc(level=LVL_FULL, tap=None):
    nc = bass.Bass("TRN2", target_bir_lowering=False)

    x_d = nc.dram_tensor("x", [T, C], F32, kind="ExternalInput")
    halo_d = nc.dram_tensor("halo", [32, C], F32, kind="ExternalInput")
    wlT_d = nc.dram_tensor("wlT", [C, F], F32, kind="ExternalInput")
    wT_d = nc.dram_tensor("wT", [C, C], F32, kind="ExternalInput")
    bias_d = nc.dram_tensor("bias", [1, C], F32, kind="ExternalInput")
    alpha_d = nc.dram_tensor("alpha", [1, 1], F32, kind="ExternalInput")
    out_d = nc.dram_tensor("out", [T, C], F32, kind="ExternalOutput")
    stage_d = nc.dram_tensor("stage", [N_WB * BPITCH + 65536], BF16, kind="Internal")
    xn1_d = nc.dram_tensor("xn1d", [2144 * C], BF16, kind="Internal")
    xn2_d = nc.dram_tensor("xn2d", [T, C], BF16, kind="Internal")

    TAPS = {
        "xn1":  ([128, NT * C], BF16),
        "wpad": ([128, 2 * WROW], BF16),
        "wet":  ([128, 2 * WET_ROWS_PAD], BF16),
        "wins": ([128, 10 * C], BF16),
        "x1":   ([128, NT * C], F32),
        "xn2T": ([128, 8 * T], BF16),
        "scm1": ([128, 17], F32),
        "scv1": ([128, 17], F32),
        "scr1": ([128, 17], F32),
        "scn1": ([128, 17], F32),
        "stats": ([128, 12], F32),
    }
    dbg_d = None
    if tap is not None:
        dbg_d = nc.dram_tensor("dbg", TAPS[tap][0], TAPS[tap][1], kind="ExternalOutput")

    al = nc.alloc_sbuf_tensor
    xs = al("xs", [128, NT * C], F32)
    wtern = al("wtern", [128, 8 * C], BF16)
    wet = al("wet", [128, 2 * WET_ROWS_PAD], BF16, is_transpose_dest=True)
    wins = al("wins", [128, 10 * C], BF16)
    wpad = al("wpad", [128, 2 * WROW], BF16)
    wlTb = al("wlTb", [128, 8 * F], BF16)
    halox = al("halox", [32, C], F32)
    xn1h = al("xn1h", [32, C], BF16)
    onesbc = al("onesbc", [128, 1], BF16)
    onesbr = al("onesbr", [1, 128], BF16)
    onesfr = al("onesfr", [1, 128], F32)
    onesfc = al("onesfc", [128, 1], F32)
    biasb = al("biasb", [1, C], BF16)
    alphas = al("alphas", [1, 1], F32)
    statsw = al("statsw", [128, 17 * 12], F32)
    mq = al("mq", [128, 2 * 68], F32)
    scm1 = al("scm1", [128, 17], F32)
    scv1 = al("scv1", [128, 17], F32)
    scr1 = al("scr1", [128, 17], F32)
    scn1 = al("scn1", [128, 17], F32)
    scm2 = al("scm2", [128, 16], F32)
    scv2 = al("scv2", [128, 16], F32)
    scr2 = al("scr2", [128, 16], F32)
    scn2 = al("scn2", [128, 16], F32)
    scq1 = al("scq1", [128, 17], F32)
    scq2 = al("scq2", [128, 16], F32)
    sumc = al("sumc", [128, 16], F32)
    recc = al("recc", [128, 16], F32)
    cst = al("cst", [128, 6], F32)
    cstb = al("cstb", [128, 8], F32)
    sc1 = al("sc1", [1, 8], F32)
    redt = al("redt", [128, 16], F32)

    ra = nc.sbuf_top - 32768
    wTf = nc.alloc_sbuf_tensor_at("wTf", [128, 8 * C], F32, offset=ra)
    xn1 = nc.alloc_sbuf_tensor_at("xn1", [128, NT * C], BF16, offset=ra)
    xn2T = nc.alloc_sbuf_tensor_at("xn2T", [128, 8 * T], BF16, offset=ra)
    rb = ra - 32768
    xn1T = nc.alloc_sbuf_tensor_at("xn1T", [128, 8 * T], BF16, offset=rb)
    wet3 = nc.alloc_sbuf_tensor_at("wet3", [128, WET_ROWS_PAD], BF16, offset=ra)
    outst = nc.alloc_sbuf_tensor_at("outst", [128, 4 * C], F32, offset=rb)
    xn2r = nc.alloc_sbuf_tensor_at("xn2r", [128, 4 * C], BF16, offset=rb + 16384)

    ps_wp = [nc.alloc_psum_tensor(f"ps_wp{i}", [128, 512], F32) for i in range(2)]
    ps_at = [nc.alloc_psum_tensor(f"ps_at{i}", [128, 1024], F32) for i in range(2)]
    ps_ff = [nc.alloc_psum_tensor(f"ps_ff{i}", [128, 512], F32) for i in range(2)]

    import contextlib
    stack = contextlib.ExitStack()
    sems = {}
    for s in ["lds", "ldx", "ldx2", "ldh", "ldw", "ldxt", "ldwl", "ldba", "x1d", "vprep", "gz", "vsq", "vabs", "mrt", "mrv", "mrf", "vcell",
              "scell", "bcv", "tch", "tern", "ln1s", "rinv1", "cols1", "xn1a", "wp",
              "expd", "wnorm", "stg", "stl", "winb", "wetb", "agg", "x1", "ln2s",
              "rinv2", "cols2", "xn2a", "x2w", "xnt", "ffn", "outs", "od", "dbgs"]:
        sems[s] = stack.enter_context(nc.semaphore(s))
    xqs = [stack.enter_context(nc.semaphore(f"xq{i}")) for i in range(4)]
    winbs = [stack.enter_context(nc.semaphore(f"winb{i}")) for i in range(16)]
    x1ts = [stack.enter_context(nc.semaphore(f"x1t{i}")) for i in range(4)]
    xnts = [stack.enter_context(nc.semaphore(f"xnt{i}")) for i in range(4)]
    stgs = [stack.enter_context(nc.semaphore(f"stg{i}")) for i in range(8)]
    wetbs = [stack.enter_context(nc.semaphore(f"wetb{i}")) for i in range(8)]
    x2ws = [stack.enter_context(nc.semaphore(f"x2w{i}")) for i in range(4)]
    x1ds = [stack.enter_context(nc.semaphore(f"x1d{i}")) for i in range(5)]

    nc._sems = sems
    block = stack.enter_context(nc.Block())

    def xs_t(i):
        return xs[:, i * C:(i + 1) * C]

    def stats_tile(vector, src_half1, src_half2, slot, np_):
        p = slice(0, np_)
        vector.bn_stats(statsw[p, slot * 12: slot * 12 + 6], src_half1)
        vector.bn_stats(statsw[p, slot * 12 + 6: slot * 12 + 12], src_half2)

    def stats_batch(vector, s0, s1, scm, scv, scq, scn):
        n = s1 - s0
        vector.drain()
        m_src4 = bass.AP(statsw, s0 * 12 + 1, [[204, 128], [12, n], [3, 4], [1, 1]])
        q_src3 = bass.AP(statsw, s0 * 12 + 2, [[204, 128], [12, n], [3, 4]])
        mc = bass.AP(mq, 0, [[136, 128], [4, n], [1, 4]])
        mc_flat = bass.AP(mq, 0, [[136, 128], [1, 4 * n]])
        vector.tensor_reduce(mc, m_src4, AX.X, ALU.add, opt_input=False, opt_output=False)
        vector.tensor_reduce(scv.unsqueeze(2), q_src3, AX.X, ALU.add, opt_input=False, opt_output=False)
        vector.drain()
        vector.tensor_reduce(scm.unsqueeze(2), mc, AX.X, ALU.add, opt_input=False, opt_output=False)
        vector.drain()
        vector.tensor_tensor(mc_flat, mc_flat, mc_flat, ALU.mult)
        vector.drain()
        vector.tensor_reduce(scq.unsqueeze(2), mc, AX.X, ALU.add, opt_input=False, opt_output=False)
        vector.tensor_scalar(scm, scm, 0.25, None, ALU.mult)
        vector.tensor_scalar(scv, scv, 1.0 / C, None, ALU.mult)
        vector.drain()
        vector.tensor_scalar(scq, scq, 0.25, None, ALU.mult)
        vector.tensor_tensor(scn, scm, scm, ALU.mult)
        vector.drain()
        vector.tensor_tensor(scv, scv, scq, ALU.add)
        vector.drain()
        vector.tensor_tensor(scv, scv, scn, ALU.subtract)
        vector.drain()

    @block.sync
    def _(sync):
        sync.dma_start(halox[:], halo_d[:]).then_inc(sems["ldh"], 16)
        for q_ in (0, 2):
            sync.dma_start(
                bass.AP(xs, 2 * q_ * C, [[NT * C, 128], [C, 2], [1, C]]),
                bass.AP(x_d, 2 * q_ * 128 * C, [[C, 128], [128 * C, 2], [1, C]]),
            ).then_inc(xqs[q_], 16)

        if level >= 3:
            sync.wait_ge(sems["gz"], 1)
            sync.dma_start(bass.AP(xn1_d, 2080 * C, [[C, 64], [1, C]]),
                           wpad[0:64, 0:C]).then_inc(sems["stl"], 16)
            sync.dma_start(bass.AP(stage_d, 131136, [[BPITCH, 8], [P_SUB, 4], [1, 64]]),
                           bass.AP(wpad, 0, [[2 * WROW, 8], [64, 4], [1, 64]])).then_inc(sems["stl"], 16)
            sync.dma_start(bass.AP(stage_d, 4 * P_SUB, [[BPITCH, 8], [1, 1536]]),
                           wpad[0:8, 0:1536]).then_inc(sems["stl"], 16)

        def wet_xbar(b):
            sync.wait_ge(stgs[b], 32)
            if b <= 1:
                sync.wait_ge(sems["tern"], 8)
            if b == 2:
                sync.wait_ge(sems["tern"], 8)
                sync.wait_ge(x1ds[2], 16)
            if b >= 3 and level >= 4:
                sync.wait_ge(sems["agg"], 4 * (b - 2))
                sync.wait_ge(wetbs[b - 3], 16)
            src = bass.AP(stage_d, b * BPITCH, [[128, WET_ROWS_PAD], [1, 128]])
            if b % 3 == 2:
                dst = wet3[:, 0:WET_ROWS_PAD]
            else:
                dst = wet[:, (b % 3) * WET_ROWS_PAD:(b % 3 + 1) * WET_ROWS_PAD]
            sync.dma_start(dst, src, transpose=True).then_inc(wetbs[b], 16)

        if level >= 2:
            n_wb = (N_WB if level >= 4 else 1) if level >= 3 else 0
            for q_ in range(4):
                sync.wait_ge(x1ds[1 + q_], 16)
                for cc in range(8):
                    src_ = bass.AP(xn1_d, (32 + 512 * q_) * C + 128 * cc, [[C, 512], [1, 128]])
                    sync.dma_start(xn1T[:, cc * T + 512 * q_: cc * T + 512 * (q_ + 1)],
                                   src_, transpose=True).then_inc(x1ts[q_], 16)
                if q_ < 3 and q_ < n_wb:
                    wet_xbar(q_)
            for b in range(3 if n_wb > 3 else n_wb, n_wb):
                wet_xbar(b)

        if level >= 5:
            for w in range(4):
                if w == 0 and level >= 4:
                    sync.wait_ge(sems["agg"], 24)
                sync.wait_ge(x2ws[w], 16)
                for cc in range(8):
                    src = bass.AP(xn2_d, 512 * w * C + 128 * cc, [[C, 512], [1, 128]])
                    sync.dma_start(xn2T[:, cc * T + 512 * w: cc * T + 512 * (w + 1)],
                                   src, transpose=True).then_inc(xnts[w], 16)



    @block.gpsimd
    def _(gpsimd):
        gpsimd.memset(wpad[:], 0.0)
        gpsimd.drain().then_inc(sems["gz"], 1)
        gpsimd.dma_start(
            bass.AP(wlTb, 0, [[8 * F, 128], [F, 8], [1, F]]),
            bass.AP(wlT_d, 0, [[F, 128], [128 * F, 8], [1, F]]),
        ).then_inc(sems["ldwl"], 16)
        gpsimd.dma_start(biasb[:], bias_d[:]).then_inc(sems["ldba"], 16)
        gpsimd.dma_start(alphas[:], alpha_d[:]).then_inc(sems["ldba"], 16)
        gpsimd.dma_start(
            bass.AP(xs, 12 * C, [[NT * C, 128], [C, 4], [1, C]]),
            bass.AP(x_d, 12 * 128 * C, [[C, 128], [128 * C, 4], [1, C]]),
        ).then_inc(sems["ldxt"], 16)

        finals = [("ldx2", 16), ("ldxt", 16), ("ldh", 16), ("ldw", 16), ("ldwl", 16), ("ldba", 32)]
        fin = {1: (sems["xn1a"], 17), 2: (sems["wnorm"], 16), 3: (wetbs[1], 16),
               4: (sems["x1"], 16), 5: (xnts[3], 128), 6: (sems["od"], 128)}[level]
        gpsimd.wait_ge(fin[0], fin[1])
        for nm, v in finals:
            gpsimd.wait_ge(sems[nm], v)
        for s_ in xqs:
            gpsimd.wait_ge(s_, 16)
        if level >= 2:
            for s_ in x1ts:
                gpsimd.wait_ge(s_, 128)
            for s_ in x1ds[1:]:
                gpsimd.wait_ge(s_, 16)
        if level >= 3:
            gpsimd.wait_ge(sems["stl"], 48)
            gpsimd.wait_ge(x1ds[0], 16)
            nwin = 4 if level < 4 else 16
            for p_ in range(nwin):
                gpsimd.wait_ge(winbs[p_], 16)
            for b_ in range(8):
                gpsimd.wait_ge(stgs[b_], 32)
            for b_ in range(2 if level < 4 else 8):
                gpsimd.wait_ge(wetbs[b_], 16)
        if level >= 5:
            for s_ in x2ws:
                gpsimd.wait_ge(s_, 16)
            for s_ in xnts:
                gpsimd.wait_ge(s_, 128)
        if level >= 6:
            gpsimd.wait_ge(sems["od"], 128)
        if tap is not None:
            src = {"xn1": xn1, "wpad": wpad, "wet": wet, "wins": wins,
                   "x1": xs, "xn2T": xn2T, "scm1": scm1, "scv1": scv1,
                   "scr1": scr1, "scn1": scn1, "stats": stats}[tap]
            gpsimd.dma_start(dbg_d.ap(), src[:]).then_inc(sems["dbgs"], 16)
            gpsimd.wait_ge(sems["dbgs"], 16)

    @block.vector
    def _(vector):
        vector.memset(cst[:, 0:1], EPS)
        vector.memset(cst[:, 1:2], -0.5)
        vector.memset(cst[:, 2:3], 0.5)
        vector.memset(cst[:, 3:4], -2.0)
        vector.memset(cst[:, 4:5], -33.0)
        vector.memset(cst[:, 5:6], -1.0)
        vector.memset(sc1[:], 0.0)
        vector.memset(scm1[:, 0:1], 0.0)
        vector.memset(scv1[:, 0:1], 1.0)
        vector.memset(scq1[:, 0:1], 0.0)
        vector.memset(scn1[:, 0:1], 0.0)
        vector.memset(onesbc[:], 1.0)
        vector.memset(onesbr[:], 1.0)
        vector.memset(onesfr[:], 1.0)
        vector.memset(onesfc[:], 1.0)
        vector.drain().then_inc(sems["vprep"], 1)
        vector.wait_ge(sems["ldh"], 16)
        vector.memset(statsw[:, 0:12], 0.0)
        vector.drain()
        stats_tile(vector, halox[:, 0:512], halox[:, 512:C], 0, 32)
        for i in range(8):
            if i % 2 == 0:
                vector.wait_ge(xqs[i // 2], 16)
            stats_tile(vector, xs[:, i * C:i * C + 512], xs[:, i * C + 512:(i + 1) * C], 1 + i, 128)
        stats_batch(vector, 0, 9, scm1[:, 0:9], scv1[:, 0:9], scq1[:, 0:9], scn1[:, 0:9])
        vector.drain().then_inc(sems["ln1s"], 1)
        vector.wait_ge(sems["rinv1"], 1)
        vector.scalar_tensor_tensor(scn1[:, 0:9], scm1[:, 0:9], -1.0, scr1[:, 0:9], ALU.mult, ALU.mult)
        vector.drain().then_inc(sems["cols1"], 1)
        vector.wait_ge(sems["ldwl"], 16)
        vector.tensor_tensor(wet[:, 0:8 * F], wlTb[:], wlTb[:], ALU.mult)
        vector.drain().then_inc(sems["vsq"], 1)
        vector.wait_ge(sems["ldw"], 16)
        for q in range(8):
            vector.tensor_reduce(redt[:, q:q + 1].unsqueeze(2), wTf[:, q * C:(q + 1) * C].unsqueeze(1),
                                 AX.X, ALU.add, apply_absolute_value=True)
        vector.drain()
        vector.tensor_reduce(redt[:, 8:9].unsqueeze(2), redt[:, 0:8].unsqueeze(1), AX.X, ALU.add)
        vector.drain().then_inc(sems["vabs"], 1)
        vector.wait_ge(sems["mrt"], 1)
        vector.tensor_reduce(sc1[0:1, 1:2].unsqueeze(2), ps_wp[1][0:1, 0:F].unsqueeze(1), AX.X, ALU.max)
        vector.wait_ge(sems["mrt"], 2)
        vector.tensor_scalar(sc1[0:1, 0:1], ps_at[0][0:1, 0:1], 0.7 / (C * C), None, ALU.mult)
        vector.wait_ge(sems["ldba"], 32)
        vector.tensor_scalar(sc1[0:1, 3:4], alphas[0:1, :], 0.5, None, ALU.mult)
        vector.drain().then_inc(sems["vcell"], 1)
        vector.wait_ge(sems["mrt"], 3)
        vector.tensor_copy(cstb[:, 0:8], ps_at[1][:, 0:8])
        vector.drain().then_inc(sems["bcv"], 1)
        for q in range(8):
            vector.wait_ge(sems["tch"], q + 1)
            vector.tensor_tensor(wtern[:, q * C:(q + 1) * C], wet[:, 0:C],
                                 wet[:, WET_ROWS_PAD:WET_ROWS_PAD + C], ALU.add).then_inc(sems["tern"], 1)

        vector.wait_ge(sems["ldx2"], 16)
        for i in range(8, NT):
            if i == 12:
                vector.wait_ge(sems["ldxt"], 16)
            stats_tile(vector, xs[:, i * C:i * C + 512], xs[:, i * C + 512:(i + 1) * C], 1 + i, 128)
        stats_batch(vector, 9, 17, scm1[:, 9:17], scv1[:, 9:17], scq1[:, 9:17], scn1[:, 9:17])
        vector.drain().then_inc(sems["ln1s"], 1)
        vector.wait_ge(sems["rinv1"], 2)
        vector.scalar_tensor_tensor(scn1[:, 9:17], scm1[:, 9:17], -1.0,
                                    scr1[:, 9:17], ALU.mult, ALU.mult)
        vector.drain().then_inc(sems["cols1"], 1)

        def norm_tile(i):
            vector.wait_ge(sems["expd"], i + 1)
            off = (i % 2) * WROW
            wg = bass.AP(wpad, off, [[2 * WROW, 128], [128, H], [1, K]])
            vector.tensor_reduce(sumc[:].unsqueeze(2), wg, AX.X, ALU.add,
                                 opt_input=False, opt_output=False)
            vector.drain()
            vector.reciprocal(recc[:], sumc[:])
            vector.drain()
            recb = bass.AP(recc, 0, [[16, 128], [1, H], [0, K]])
            vector.tensor_tensor(wg, wg, recb, ALU.mult).then_inc(sems["wnorm"], 1)

        def evac_pair(i):
            vector.wait_ge(sems["agg"], 2 * (i + 1))
            vector.tensor_tensor(xs_t(i), ps_at[i % 2][:], xs_t(i), ALU.add).then_inc(sems["x1"], 1)
            if level >= 5:
                vector.drain()
                stats_tile(vector, xs[:, i * C:i * C + 512], xs[:, i * C + 512:(i + 1) * C], i, 128)
                if i == 7:
                    stats_batch(vector, 0, 8, scm2[:, 0:8], scv2[:, 0:8], scq2[:, 0:8], scn2[:, 0:8])
                    vector.drain().then_inc(sems["ln2s"], 1)

        if level >= 2:
            for i in range(NT):
                norm_tile(i)
            if level >= 4:
                for i in range(NT):
                    evac_pair(i)

        if level >= 5:
            stats_batch(vector, 8, 16, scm2[:, 8:16], scv2[:, 8:16], scq2[:, 8:16], scn2[:, 8:16])
            vector.drain().then_inc(sems["ln2s"], 1)
            a2b8 = bass.AP(cstb, 3, [[8, 128], [0, 8]])
            vector.wait_ge(sems["rinv2"], 1)
            vector.tensor_tensor(scr2[:, 0:8], scr2[:, 0:8], a2b8, ALU.mult)
            vector.drain()
            vector.scalar_tensor_tensor(scn2[:, 0:8], scm2[:, 0:8], -1.0, scr2[:, 0:8], ALU.mult, ALU.mult)
            vector.drain().then_inc(sems["cols2"], 1)
            vector.wait_ge(sems["rinv2"], 2)
            vector.tensor_tensor(scr2[:, 8:16], scr2[:, 8:16], a2b8, ALU.mult)
            vector.drain()
            vector.scalar_tensor_tensor(scn2[:, 8:16], scm2[:, 8:16], -1.0, scr2[:, 8:16], ALU.mult, ALU.mult)
            vector.drain().then_inc(sems["cols2"], 1)

        if level >= 6:
            for i in range(NT):
                fb = [ps_ff[0], ps_ff[1]] if i % 2 == 0 else [ps_wp[0], ps_wp[1]]
                for hf in range(2):
                    vector.wait_ge(sems["ffn"], 2 * i + hf + 1)
                    if i >= 4:
                        vector.wait_ge(sems["od"], 16 * ((i - 4) // 2 + 1))
                    vector.tensor_tensor(
                        outst[:, (i % 4) * C + hf * 512:(i % 4) * C + hf * 512 + 512],
                        fb[hf][:], xs[:, i * C + hf * 512: i * C + hf * 512 + 512],
                        ALU.add).then_inc(sems["outs"], 1)

    @block.scalar
    def _(scalar):
        for q_ in (1, 3):
            scalar.dma_start(
                bass.AP(xs, 2 * q_ * C, [[NT * C, 128], [C, 2], [1, C]]),
                bass.AP(x_d, 2 * q_ * 128 * C, [[C, 128], [128 * C, 2], [1, C]]),
            ).then_inc(xqs[q_], 16)
        scalar.dma_start(
            bass.AP(wTf, 0, [[8 * C, 128], [C, 8], [1, C]]),
            bass.AP(wT_d, 0, [[C, 128], [128 * C, 8], [1, C]]),
        ).then_inc(sems["ldw"], 16)
        scalar.dma_start(
            bass.AP(xs, 8 * C, [[NT * C, 128], [C, 4], [1, C]]),
            bass.AP(x_d, 8 * 128 * C, [[C, 128], [128 * C, 4], [1, C]]),
        ).then_inc(sems["ldx2"], 16)
        scalar.wait_ge(sems["ln1s"], 1)
        scalar.activation(scr1[:, 0:9], scv1[:, 0:9], ACT.Ln, bias=cst[:, 0:1])
        scalar.drain()
        scalar.activation(scr1[:, 0:9], scr1[:, 0:9], ACT.Exp, scale=cst[:, 1:2])
        scalar.drain().then_inc(sems["rinv1"], 1)
        scalar.wait_ge(sems["cols1"], 1)
        scalar.activation(xn1h[:], halox[:], ACT.Identity, bias=scn1[0:32, 0:1], scale=scr1[0:32, 0:1])
        scalar.drain().then_inc(sems["xn1a"], 1)
        if level >= 3:
            scalar.dma_start(bass.AP(xn1_d, 0, [[C, 32], [1, C]]),
                             xn1h[:]).then_inc(x1ds[0], 16)
        for i_e in range(4):
            scalar.activation(wins[:, (6 + i_e) * C:(7 + i_e) * C], xs_t(i_e), ACT.Identity,
                              bias=scn1[:, i_e + 1:i_e + 2],
                              scale=scr1[:, i_e + 1:i_e + 2]).then_inc(sems["xn1a"], 1)
        if level >= 2:
            scalar.drain()
            scalar.dma_start(
                bass.AP(xn1_d, 32 * C, [[C, 128], [128 * C, 4], [1, C]]),
                bass.AP(wins, 6 * C, [[10 * C, 128], [C, 4], [1, C]]),
            ).then_inc(x1ds[1], 16)
        scalar.wait_ge(sems["vcell"], 1)
        scalar.wait_ge(sems["vprep"], 1)
        scalar.activation(sc1[0:1, 2:3], sc1[0:1, 1:2], ACT.Ln)
        scalar.drain()
        scalar.activation(sc1[0:1, 2:3], sc1[0:1, 2:3], ACT.Exp, scale=cst[0:1, 2:3])
        scalar.drain()
        scalar.activation(sc1[0:1, 2:3], sc1[0:1, 2:3], ACT.Identity,
                          scale=cst[0:1, 4:5], bias=cst[0:1, 3:4])
        scalar.activation(sc1[0:1, 4:5], sc1[0:1, 0:1], ACT.Identity, scale=cst[0:1, 5:6])
        scalar.drain()

        scalar.sem_inc(sems["scell"], 1)
        scalar.wait_ge(sems["bcv"], 1)
        for q in range(8):
            if q > 0:
                scalar.wait_ge(sems["tern"], q)
            scalar.sign(wet[:, 0:C], wTf[:, q * C:(q + 1) * C], bias=cstb[:, 4:5])
            scalar.sign(wet[:, WET_ROWS_PAD:WET_ROWS_PAD + C], wTf[:, q * C:(q + 1) * C],
                        bias=cstb[:, 0:1]).then_inc(sems["tch"], 1)

        wpbanks_s = [ps_wp[0], ps_wp[1], ps_ff[0], ps_ff[1]]

        def exp_act(i):
            scalar.wait_ge(sems["wp"], i + 1)
            if i >= 2 and level >= 3:
                scalar.wait_ge(stgs[i // 2 - 1], 32)
            if i >= 2:
                scalar.wait_ge(sems["wnorm"], i - 1)
            if i < 2:
                scalar.wait_ge(sems["gz"], 1)
                if level >= 3:
                    scalar.wait_ge(sems["stl"], 48)
            off = (i % 2) * WROW
            outap = bass.AP(wpad, off, [[2 * WROW, 128], [128, H], [1, K]])
            inap = bass.AP(wpbanks_s[i % 4], 0, [[512, 128], [K, H], [1, K]])
            scalar.activation(outap, inap, ACT.Exp,
                              bias=cstb[:, 2:3]).then_inc(sems["expd"], 1)

        def win_read(p):
            if level < 4 and p >= 5:
                return
            wmax = min(3, (128 * p + 225) // 512)
            scalar.wait_ge(x1ds[0], 16)
            for w_ in range(wmax + 1):
                scalar.wait_ge(x1ds[1 + w_], 16)
            if p >= 5 and level >= 4:
                scalar.wait_ge(sems["agg"], 2 * p - 8)
            if p == 15:
                scalar.wait_ge(sems["stl"], 48)
            s = (2 * p) % 10
            scalar.dma_start(
                bass.AP(wins, s * C, [[10 * C, 128], [C, 2], [1, C]]),
                bass.AP(xn1_d, (128 * p + 2) * C, [[C, 128], [64 * C, 2], [1, C]]),
            ).then_inc(winbs[p], 16)

        def stage_wr(b):
            scalar.wait_ge(sems["wnorm"], 2 * b + 2)
            scalar.dma_start(
                bass.AP(stage_d, b * BPITCH, [[WROW, 64], [2 * P_SUB, 2], [1, WROW]]),
                bass.AP(wpad, 0, [[2 * WROW, 64], [WROW, 2], [1, WROW]]),
            ).then_inc(stgs[b], 16)
            scalar.dma_start(
                bass.AP(stage_d, b * BPITCH + P_SUB, [[WROW, 64], [2 * P_SUB, 2], [1, WROW]]),
                bass.AP(wpad, 64 * 2 * WROW, [[2 * WROW, 64], [WROW, 2], [1, WROW]]),
            ).then_inc(stgs[b], 16)

        nextp = 0
        for i in range(4, NT):
            if i == 5:
                scalar.wait_ge(sems["ln1s"], 2)
                scalar.activation(scr1[:, 9:17], scv1[:, 9:17], ACT.Ln, bias=cst[:, 0:1])
                scalar.drain()
                scalar.activation(scr1[:, 9:17], scr1[:, 9:17], ACT.Exp, scale=cst[:, 1:2])
                scalar.drain().then_inc(sems["rinv1"], 1)
            if i == 8:
                scalar.wait_ge(sems["cols1"], 2)
            scalar.activation(xn1[:, i * C:(i + 1) * C], xs_t(i), ACT.Identity,
                              bias=scn1[:, i + 1:i + 2], scale=scr1[:, i + 1:i + 2]).then_inc(sems["xn1a"], 1)
            if level >= 2 and i % 4 == 3:
                w = i // 4
                scalar.drain()
                scalar.dma_start(
                    bass.AP(xn1_d, (32 + 512 * w) * C, [[C, 128], [128 * C, 4], [1, C]]),
                    bass.AP(xn1, 4 * w * C, [[NT * C, 128], [C, 4], [1, C]]),
                ).then_inc(x1ds[1 + w], 16)
            if level >= 2:
                j = i - 4
                exp_act(j)
                if level >= 3 and j % 2 == 1:
                    stage_wr(j // 2)
            if level >= 2 and i % 4 == 3:
                w = i // 4
                while nextp < 5 and 128 * nextp + 258 <= 32 + 512 * (w + 1):
                    win_read(nextp)
                    nextp += 1
        if level >= 2:
            exp_act(12)
            exp_act(13)
            if level >= 3:
                stage_wr(6)
            exp_act(14)
            exp_act(15)
            if level >= 3:
                stage_wr(7)
        if level >= 3:
            while nextp < 16:
                win_read(nextp)
                nextp += 1

        if level >= 5:
            scalar.wait_ge(sems["ln2s"], 1)
            scalar.activation(scr2[:, 0:8], scv2[:, 0:8], ACT.Ln, bias=cst[:, 0:1])
            scalar.drain()
            scalar.activation(scr2[:, 0:8], scr2[:, 0:8], ACT.Exp, scale=cst[:, 1:2])
            scalar.drain().then_inc(sems["rinv2"], 1)
            scalar.wait_ge(sems["cols2"], 1)
            for i in range(NT):
                if i == 8:
                    scalar.wait_ge(sems["ln2s"], 2)
                    scalar.activation(scr2[:, 8:16], scv2[:, 8:16], ACT.Ln, bias=cst[:, 0:1])
                    scalar.drain()
                    scalar.activation(scr2[:, 8:16], scr2[:, 8:16], ACT.Exp, scale=cst[:, 1:2])
                    scalar.drain().then_inc(sems["rinv2"], 1)
                    scalar.wait_ge(sems["cols2"], 2)
                if i >= 4:
                    scalar.wait_ge(x2ws[i // 4 - 1], 16)
                scalar.activation(xn2r[:, (i % 4) * C:(i % 4 + 1) * C], xs_t(i), ACT.Identity,
                                  bias=scn2[:, i:i + 1], scale=scr2[:, i:i + 1]).then_inc(sems["xn2a"], 1)
                if i % 4 == 3:
                    w_ = i // 4
                    scalar.drain()
                    scalar.dma_start(
                        bass.AP(xn2_d, 4 * w_ * 128 * C, [[C, 128], [128 * C, 4], [1, C]]),
                        bass.AP(xn2r, 0, [[4 * C, 128], [C, 4], [1, C]]),
                    ).then_inc(x2ws[w_], 16)
            if level >= 6:
                for w in range(8):
                    scalar.wait_ge(sems["outs"], 4 * (w + 1))
                    if w > 0:
                        scalar.wait_ge(sems["od"], 16 * w)
                    scalar.dma_start(
                        bass.AP(out_d, 2 * w * 128 * C, [[C, 128], [128 * C, 2], [1, C]]),
                        bass.AP(outst, ((2 * w) % 4) * C, [[4 * C, 128], [C, 2], [1, C]]),
                    ).then_inc(sems["od"], 16)


    @block.tensor
    def _(tensor):
        tensor.wait_ge(sems["ldwl"], 16)
        tensor.wait_ge(sems["vprep"], 1)
        tensor.wait_ge(sems["vsq"], 1)
        for cc in range(8):
            mm = tensor.matmul(ps_wp[1][0:1, 0:F], onesbc[:], wet[:, cc * F:(cc + 1) * F],
                               start=(cc == 0), stop=(cc == 7))
        mm.then_inc(sems["mrt"], 1)
        tensor.wait_ge(sems["vabs"], 1)
        tensor.matmul(ps_at[0][0:1, 0:1], onesfc[:], redt[:, 8:9],
                      start=True, stop=True).then_inc(sems["mrt"], 1)
        tensor.wait_ge(sems["scell"], 1)
        tensor.matmul(ps_at[1][:, 0:8], onesfr[:], sc1[0:1, :],
                      start=True, stop=True).then_inc(sems["mrt"], 1)

        wpbanks = [ps_wp[0], ps_wp[1], ps_ff[0], ps_ff[1]]

        def wproj_tile(i):
            if i % 4 == 0:
                tensor.wait_ge(x1ts[i // 4], 128)
            if i >= 4:
                tensor.wait_ge(sems["expd"], i - 3)
            pw = wpbanks[i % 4]
            for cc in range(8):
                mm = tensor.matmul(pw[:, 0:F], xn1T[:, cc * T + i * 128:cc * T + (i + 1) * 128],
                                   wlTb[:, cc * F:(cc + 1) * F], start=(cc == 0), stop=(cc == 7))
            mm.then_inc(sems["wp"], 1)

        def agg_pair(i):
            b = i // 2
            tensor.wait_ge(wetbs[b], 16)
            tensor.wait_ge(winbs[i], 16)
            if i == 0:
                tensor.wait_ge(sems["vcell"], 1)
            if i == 1:
                tensor.wait_ge(sems["bcv"], 1)
            if i >= 2:
                tensor.wait_ge(sems["x1"], i - 1)
            pa = ps_at[i % 2]
            for jj in range(2):
                j = 2 * i + jj
                bb = b % 3
                colbase = (j - 4 * b) * 1025
                s = (2 * i) % 10 + jj
                for h in range(H):
                    if bb == 2:
                        lhsT = bass.AP(wet3, colbase + h, [[WET_ROWS_PAD, 128], [16, 64]])
                    else:
                        lhsT = bass.AP(wet, bb * WET_ROWS_PAD + colbase + h,
                                       [[2 * WET_ROWS_PAD, 128], [16, 64]])
                    rhs = wins[:, s * C + h * R:s * C + (h + 1) * R]
                    mm = tensor.matmul(pa[64 * jj:64 * jj + 64, h * R:(h + 1) * R], lhsT, rhs,
                                       start=True, stop=True)
                mm.then_inc(sems["agg"], 1)

        if level >= 2:
            tensor.wait_ge(sems["vcell"], 1)
            for i in range(NT):
                wproj_tile(i)
            if level >= 4:
                for i in range(NT):
                    agg_pair(i)

        if level >= 6:
            tensor.wait_ge(sems["tern"], 8)
            tensor.wait_ge(sems["ldba"], 32)
            for i in range(NT):
                if i % 4 == 0:
                    tensor.wait_ge(xnts[i // 4], 128)
                if i >= 2:
                    tensor.wait_ge(sems["outs"], 2 * (i - 2) + 2)
                fb = [ps_ff[0], ps_ff[1]] if i % 2 == 0 else [ps_wp[0], ps_wp[1]]
                for cc in range(8):
                    lhs = xn2T[:, cc * T + i * 128:cc * T + (i + 1) * 128]
                    tensor.matmul(fb[0][:], lhs, wtern[:, cc * C:cc * C + 512],
                                  start=(cc == 0), stop=False)
                    tensor.matmul(fb[1][:], lhs, wtern[:, cc * C + 512:cc * C + 1024],
                                  start=(cc == 0), stop=False)
                tensor.matmul(fb[0][:], onesbr[0:1, :], biasb[0:1, 0:512],
                              start=False, stop=True).then_inc(sems["ffn"], 1)
                tensor.matmul(fb[1][:], onesbr[0:1, :], biasb[0:1, 512:1024],
                              start=False, stop=True).then_inc(sems["ffn"], 1)

    stack.close()
    return nc


def make_inputs(x, wl, wf, al_, bi):
    x = np.asarray(x, dtype=np.float32)
    wlT = np.ascontiguousarray(np.asarray(wl, dtype=np.float32).T)
    wT = np.ascontiguousarray(np.asarray(wf, dtype=np.float32).T)
    al_ = np.asarray(al_, dtype=np.float32).reshape(1, 1)
    bi = np.asarray(bi, dtype=np.float32).reshape(1, C)
    in_maps = []
    for c in range(NCORES):
        seq, half = c // 2, c % 2
        xs_ = np.ascontiguousarray(x[seq, half * T:(half + 1) * T])
        if half == 0:
            haloa = np.zeros((32, C), dtype=np.float32)
        else:
            haloa = np.ascontiguousarray(x[seq, half * T - 32:half * T])
        in_maps.append({"x": xs_, "halo": haloa, "wlT": wlT, "wT": wT,
                        "bias": bi, "alpha": al_})
    return in_maps


def kernel(L0_attn_x, L0_attn_w_lin, L0_ffn_weight, L0_ffn_alpha, L0_ffn_bias):
    in_maps = make_inputs(L0_attn_x, L0_attn_w_lin, L0_ffn_weight, L0_ffn_alpha, L0_ffn_bias)
    nc = build_nc(level=LVL_FULL)
    res = run_bass_kernel_spmd(nc, in_maps, core_ids=list(range(NCORES)))
    out = np.empty((4, 4096, C), dtype=np.float32)
    for c in range(NCORES):
        seq, half = c // 2, c % 2
        out[seq, half * T:(half + 1) * T] = res.results[c]["out"]
    return out


# version 21 (final)
# speedup vs baseline: 1.0156x; 1.0156x over previous
import numpy as np
import concourse.bass as bass
import concourse.mybir as mybir
from concourse.bass_utils import run_bass_kernel_spmd

F32 = mybir.dt.float32
BF16 = mybir.dt.bfloat16
ALU = mybir.AluOpType
ACT = mybir.ActivationFunctionType
AX = mybir.AxisListType

NCORES = 8
T = 2048
C = 1024
H = 16
K = 31
R = 64
F = H * K
NT = 16
NSUB = 32
EPS = 1e-5

WROW = 16 * 128 + 1
P_SUB = 128 * 1025
WET_BATCH = 4
WET_ROWS_PAD = 4112
N_WB = NSUB // WET_BATCH
BPITCH = 4 * P_SUB + 1536

LVL_FULL = 6


def build_nc(level=LVL_FULL, tap=None):
    nc = bass.Bass("TRN2", target_bir_lowering=False)

    x_d = nc.dram_tensor("x", [T, C], F32, kind="ExternalInput")
    halo_d = nc.dram_tensor("halo", [32, C], F32, kind="ExternalInput")
    wlT_d = nc.dram_tensor("wlT", [C, F], F32, kind="ExternalInput")
    wT_d = nc.dram_tensor("wT", [C, C], F32, kind="ExternalInput")
    bias_d = nc.dram_tensor("bias", [1, C], F32, kind="ExternalInput")
    alpha_d = nc.dram_tensor("alpha", [1, 1], F32, kind="ExternalInput")
    out_d = nc.dram_tensor("out", [T, C], F32, kind="ExternalOutput")
    stage_d = nc.dram_tensor("stage", [N_WB * BPITCH + 65536], BF16, kind="Internal")
    xn1_d = nc.dram_tensor("xn1d", [2144 * C], BF16, kind="Internal")
    xn2_d = nc.dram_tensor("xn2d", [T, C], BF16, kind="Internal")

    TAPS = {
        "xn1":  ([128, NT * C], BF16),
        "wpad": ([128, 2 * WROW], BF16),
        "wet":  ([128, 2 * WET_ROWS_PAD], BF16),
        "wins": ([128, 10 * C], BF16),
        "x1":   ([128, NT * C], F32),
        "xn2T": ([128, 8 * T], BF16),
        "scm1": ([128, 17], F32),
        "scv1": ([128, 17], F32),
        "scr1": ([128, 17], F32),
        "scn1": ([128, 17], F32),
        "stats": ([128, 12], F32),
    }
    dbg_d = None
    if tap is not None:
        dbg_d = nc.dram_tensor("dbg", TAPS[tap][0], TAPS[tap][1], kind="ExternalOutput")

    al = nc.alloc_sbuf_tensor
    xs = al("xs", [128, NT * C], F32)
    wtern = al("wtern", [128, 8 * C], BF16)
    wet = al("wet", [128, 2 * WET_ROWS_PAD], BF16, is_transpose_dest=True)
    wins = al("wins", [128, 10 * C], BF16)
    wpad = al("wpad", [128, 2 * WROW], BF16)
    wlTb = al("wlTb", [128, 8 * F], BF16)
    halox = al("halox", [32, C], F32)
    xn1h = al("xn1h", [32, C], BF16)
    onesbc = al("onesbc", [128, 1], BF16)
    onesbr = al("onesbr", [1, 128], BF16)
    onesfr = al("onesfr", [1, 128], F32)
    onesfc = al("onesfc", [128, 1], F32)
    biasb = al("biasb", [1, C], BF16)
    alphas = al("alphas", [1, 1], F32)
    statsw = al("statsw", [128, 17 * 12], F32)
    mq = al("mq", [128, 2 * 68], F32)
    scm1 = al("scm1", [128, 17], F32)
    scv1 = al("scv1", [128, 17], F32)
    scr1 = al("scr1", [128, 17], F32)
    scn1 = al("scn1", [128, 17], F32)
    scm2 = al("scm2", [128, 16], F32)
    scv2 = al("scv2", [128, 16], F32)
    scr2 = al("scr2", [128, 16], F32)
    scn2 = al("scn2", [128, 16], F32)
    scq1 = al("scq1", [128, 17], F32)
    scq2 = al("scq2", [128, 16], F32)
    sumc = al("sumc", [128, 16], F32)
    recc = al("recc", [128, 16], F32)
    cst = al("cst", [128, 6], F32)
    cstb = al("cstb", [128, 8], F32)
    sc1 = al("sc1", [1, 8], F32)
    redt = al("redt", [128, 16], F32)

    ra = nc.sbuf_top - 32768
    wTf = nc.alloc_sbuf_tensor_at("wTf", [128, 8 * C], F32, offset=ra)
    xn1 = nc.alloc_sbuf_tensor_at("xn1", [128, NT * C], BF16, offset=ra)
    xn2T = nc.alloc_sbuf_tensor_at("xn2T", [128, 8 * T], BF16, offset=ra)
    rb = ra - 32768
    xn1T = nc.alloc_sbuf_tensor_at("xn1T", [128, 8 * T], BF16, offset=rb)
    wet3 = nc.alloc_sbuf_tensor_at("wet3", [128, WET_ROWS_PAD], BF16, offset=ra)
    outst = nc.alloc_sbuf_tensor_at("outst", [128, 4 * C], F32, offset=rb)
    xn2r = nc.alloc_sbuf_tensor_at("xn2r", [128, 4 * C], BF16, offset=rb + 16384)

    ps_wp = [nc.alloc_psum_tensor(f"ps_wp{i}", [128, 512], F32) for i in range(2)]
    ps_at = [nc.alloc_psum_tensor(f"ps_at{i}", [128, 1024], F32) for i in range(2)]
    ps_ff = [nc.alloc_psum_tensor(f"ps_ff{i}", [128, 512], F32) for i in range(2)]

    import contextlib
    stack = contextlib.ExitStack()
    sems = {}
    for s in ["lds", "ldx", "ldx2", "ldh", "ldw", "ldxt", "ldwl", "ldba", "x1d", "vprep", "gz", "vsq", "vabs", "mrt", "mrv", "mrf", "vcell",
              "scell", "bcv", "tch", "tern", "ln1s", "rinv1", "cols1", "xn1a", "wp",
              "expd", "wnorm", "stg", "stl", "winb", "wetb", "agg", "x1", "ln2s",
              "rinv2", "cols2", "xn2a", "x2w", "xnt", "ffn", "outs", "od", "dbgs"]:
        sems[s] = stack.enter_context(nc.semaphore(s))
    xqs = [stack.enter_context(nc.semaphore(f"xq{i}")) for i in range(4)]
    winbs = [stack.enter_context(nc.semaphore(f"winb{i}")) for i in range(16)]
    x1ts = [stack.enter_context(nc.semaphore(f"x1t{i}")) for i in range(4)]
    xnts = [stack.enter_context(nc.semaphore(f"xnt{i}")) for i in range(4)]
    stgs = [stack.enter_context(nc.semaphore(f"stg{i}")) for i in range(8)]
    wetbs = [stack.enter_context(nc.semaphore(f"wetb{i}")) for i in range(8)]
    x2ws = [stack.enter_context(nc.semaphore(f"x2w{i}")) for i in range(4)]
    x1ds = [stack.enter_context(nc.semaphore(f"x1d{i}")) for i in range(5)]

    nc._sems = sems
    block = stack.enter_context(nc.Block())

    def xs_t(i):
        return xs[:, i * C:(i + 1) * C]

    def stats_tile(vector, src_half1, src_half2, slot, np_):
        p = slice(0, np_)
        vector.bn_stats(statsw[p, slot * 12: slot * 12 + 6], src_half1)
        vector.bn_stats(statsw[p, slot * 12 + 6: slot * 12 + 12], src_half2)

    def stats_batch(vector, s0, s1, scm, scv, scq, scn):
        n = s1 - s0
        vector.drain()
        m_src4 = bass.AP(statsw, s0 * 12 + 1, [[204, 128], [12, n], [3, 4], [1, 1]])
        q_src3 = bass.AP(statsw, s0 * 12 + 2, [[204, 128], [12, n], [3, 4]])
        mc = bass.AP(mq, 0, [[136, 128], [4, n], [1, 4]])
        mc_flat = bass.AP(mq, 0, [[136, 128], [1, 4 * n]])
        vector.tensor_reduce(mc, m_src4, AX.X, ALU.add, opt_input=False, opt_output=False)
        vector.tensor_reduce(scv.unsqueeze(2), q_src3, AX.X, ALU.add, opt_input=False, opt_output=False)
        vector.drain()
        vector.tensor_reduce(scm.unsqueeze(2), mc, AX.X, ALU.add, opt_input=False, opt_output=False)
        vector.drain()
        vector.tensor_tensor(mc_flat, mc_flat, mc_flat, ALU.mult)
        vector.drain()
        vector.tensor_reduce(scq.unsqueeze(2), mc, AX.X, ALU.add, opt_input=False, opt_output=False)
        vector.tensor_scalar(scm, scm, 0.25, None, ALU.mult)
        vector.tensor_scalar(scv, scv, 1.0 / C, None, ALU.mult)
        vector.drain()
        vector.tensor_scalar(scq, scq, 0.25, None, ALU.mult)
        vector.tensor_tensor(scn, scm, scm, ALU.mult)
        vector.drain()
        vector.tensor_tensor(scv, scv, scq, ALU.add)
        vector.drain()
        vector.tensor_tensor(scv, scv, scn, ALU.subtract)
        vector.drain()

    @block.sync
    def _(sync):
        sync.dma_start(halox[:], halo_d[:]).then_inc(sems["ldh"], 16)
        for q_ in (0, 2):
            sync.dma_start(
                bass.AP(xs, 2 * q_ * C, [[NT * C, 128], [C, 2], [1, C]]),
                bass.AP(x_d, 2 * q_ * 128 * C, [[C, 128], [128 * C, 2], [1, C]]),
            ).then_inc(xqs[q_], 16)

        if level >= 3:
            sync.wait_ge(sems["gz"], 1)
            sync.dma_start(bass.AP(xn1_d, 2080 * C, [[C, 64], [1, C]]),
                           wpad[0:64, 0:C]).then_inc(sems["stl"], 16)
            sync.dma_start(bass.AP(stage_d, 131136, [[BPITCH, 8], [P_SUB, 4], [1, 64]]),
                           bass.AP(wpad, 0, [[2 * WROW, 8], [64, 4], [1, 64]])).then_inc(sems["stl"], 16)
            sync.dma_start(bass.AP(stage_d, 4 * P_SUB, [[BPITCH, 8], [1, 1536]]),
                           wpad[0:8, 0:1536]).then_inc(sems["stl"], 16)

        def wet_xbar(b):
            sync.wait_ge(stgs[b], 32)
            if b <= 1:
                sync.wait_ge(sems["tern"], 8)
            if b == 2:
                sync.wait_ge(sems["tern"], 8)
                sync.wait_ge(x1ds[2], 16)
            if b >= 3 and level >= 4:
                sync.wait_ge(sems["agg"], 4 * (b - 2))
                sync.wait_ge(wetbs[b - 3], 16)
            src = bass.AP(stage_d, b * BPITCH, [[128, WET_ROWS_PAD], [1, 128]])
            if b % 3 == 2:
                dst = wet3[:, 0:WET_ROWS_PAD]
            else:
                dst = wet[:, (b % 3) * WET_ROWS_PAD:(b % 3 + 1) * WET_ROWS_PAD]
            sync.dma_start(dst, src, transpose=True).then_inc(wetbs[b], 16)

        if level >= 2:
            n_wb = (N_WB if level >= 4 else 1) if level >= 3 else 0
            for q_ in range(4):
                sync.wait_ge(x1ds[1 + q_], 16)
                for cc in range(8):
                    src_ = bass.AP(xn1_d, (32 + 512 * q_) * C + 128 * cc, [[C, 512], [1, 128]])
                    sync.dma_start(xn1T[:, cc * T + 512 * q_: cc * T + 512 * (q_ + 1)],
                                   src_, transpose=True).then_inc(x1ts[q_], 16)
                if q_ < 3 and q_ < n_wb:
                    wet_xbar(q_)
            for b in range(3 if n_wb > 3 else n_wb, n_wb):
                wet_xbar(b)

        if level >= 5:
            for w in range(4):
                if w == 0 and level >= 4:
                    sync.wait_ge(sems["agg"], 24)
                sync.wait_ge(x2ws[w], 16)
                for cc in range(8):
                    src = bass.AP(xn2_d, 512 * w * C + 128 * cc, [[C, 512], [1, 128]])
                    sync.dma_start(xn2T[:, cc * T + 512 * w: cc * T + 512 * (w + 1)],
                                   src, transpose=True).then_inc(xnts[w], 16)



    @block.gpsimd
    def _(gpsimd):
        gpsimd.memset(wpad[:], 0.0)
        gpsimd.drain().then_inc(sems["gz"], 1)
        gpsimd.dma_start(
            bass.AP(wlTb, 0, [[8 * F, 128], [F, 8], [1, F]]),
            bass.AP(wlT_d, 0, [[F, 128], [128 * F, 8], [1, F]]),
        ).then_inc(sems["ldwl"], 16)
        gpsimd.dma_start(biasb[:], bias_d[:]).then_inc(sems["ldba"], 16)
        gpsimd.dma_start(alphas[:], alpha_d[:]).then_inc(sems["ldba"], 16)
        gpsimd.dma_start(
            bass.AP(xs, 12 * C, [[NT * C, 128], [C, 4], [1, C]]),
            bass.AP(x_d, 12 * 128 * C, [[C, 128], [128 * C, 4], [1, C]]),
        ).then_inc(sems["ldxt"], 16)

        finals = [("ldx2", 16), ("ldxt", 16), ("ldh", 16), ("ldw", 16), ("ldwl", 16), ("ldba", 32)]
        fin = {1: (sems["xn1a"], 17), 2: (sems["wnorm"], 16), 3: (wetbs[1], 16),
               4: (sems["x1"], 16), 5: (xnts[3], 128), 6: (sems["od"], 128)}[level]
        gpsimd.wait_ge(fin[0], fin[1])
        for nm, v in finals:
            gpsimd.wait_ge(sems[nm], v)
        for s_ in xqs:
            gpsimd.wait_ge(s_, 16)
        if level >= 2:
            for s_ in x1ts:
                gpsimd.wait_ge(s_, 128)
            for s_ in x1ds[1:]:
                gpsimd.wait_ge(s_, 16)
        if level >= 3:
            gpsimd.wait_ge(sems["stl"], 48)
            gpsimd.wait_ge(x1ds[0], 16)
            nwin = 4 if level < 4 else 16
            for p_ in range(nwin):
                gpsimd.wait_ge(winbs[p_], 16)
            for b_ in range(8):
                gpsimd.wait_ge(stgs[b_], 32)
            for b_ in range(2 if level < 4 else 8):
                gpsimd.wait_ge(wetbs[b_], 16)
        if level >= 5:
            for s_ in x2ws:
                gpsimd.wait_ge(s_, 16)
            for s_ in xnts:
                gpsimd.wait_ge(s_, 128)
        if level >= 6:
            gpsimd.wait_ge(sems["od"], 128)
        if tap is not None:
            src = {"xn1": xn1, "wpad": wpad, "wet": wet, "wins": wins,
                   "x1": xs, "xn2T": xn2T, "scm1": scm1, "scv1": scv1,
                   "scr1": scr1, "scn1": scn1, "stats": stats}[tap]
            gpsimd.dma_start(dbg_d.ap(), src[:]).then_inc(sems["dbgs"], 16)
            gpsimd.wait_ge(sems["dbgs"], 16)

    @block.vector
    def _(vector):
        vector.memset(cst[:, 0:1], EPS)
        vector.memset(cst[:, 1:2], -0.5)
        vector.memset(cst[:, 2:3], 0.5)
        vector.memset(cst[:, 3:4], -2.0)
        vector.memset(cst[:, 4:5], -33.0)
        vector.memset(cst[:, 5:6], -1.0)
        vector.memset(sc1[:], 0.0)
        vector.memset(scm1[:, 0:1], 0.0)
        vector.memset(scv1[:, 0:1], 1.0)
        vector.memset(scq1[:, 0:1], 0.0)
        vector.memset(scn1[:, 0:1], 0.0)
        vector.memset(onesbc[:], 1.0)
        vector.memset(onesbr[:], 1.0)
        vector.memset(onesfr[:], 1.0)
        vector.memset(onesfc[:], 1.0)
        vector.drain().then_inc(sems["vprep"], 1)
        vector.wait_ge(sems["ldh"], 16)
        vector.memset(statsw[:, 0:12], 0.0)
        vector.drain()
        stats_tile(vector, halox[:, 0:512], halox[:, 512:C], 0, 32)
        for i in range(8):
            if i % 2 == 0:
                vector.wait_ge(xqs[i // 2], 16)
            stats_tile(vector, xs[:, i * C:i * C + 512], xs[:, i * C + 512:(i + 1) * C], 1 + i, 128)
        stats_batch(vector, 0, 9, scm1[:, 0:9], scv1[:, 0:9], scq1[:, 0:9], scn1[:, 0:9])
        vector.drain().then_inc(sems["ln1s"], 1)
        vector.wait_ge(sems["rinv1"], 1)
        vector.scalar_tensor_tensor(scn1[:, 0:9], scm1[:, 0:9], -1.0, scr1[:, 0:9], ALU.mult, ALU.mult)
        vector.drain().then_inc(sems["cols1"], 1)
        vector.wait_ge(sems["ldwl"], 16)
        vector.tensor_tensor(wet[:, 0:8 * F], wlTb[:], wlTb[:], ALU.mult)
        vector.drain().then_inc(sems["vsq"], 1)
        vector.wait_ge(sems["ldw"], 16)
        for q in range(8):
            vector.tensor_reduce(redt[:, q:q + 1].unsqueeze(2), wTf[:, q * C:(q + 1) * C].unsqueeze(1),
                                 AX.X, ALU.add, apply_absolute_value=True)
        vector.drain()
        vector.tensor_reduce(redt[:, 8:9].unsqueeze(2), redt[:, 0:8].unsqueeze(1), AX.X, ALU.add)
        vector.drain().then_inc(sems["vabs"], 1)
        vector.wait_ge(sems["mrt"], 1)
        vector.tensor_reduce(sc1[0:1, 1:2].unsqueeze(2), ps_wp[1][0:1, 0:F].unsqueeze(1), AX.X, ALU.max)
        vector.wait_ge(sems["mrt"], 2)
        vector.tensor_scalar(sc1[0:1, 0:1], ps_at[0][0:1, 0:1], 0.7 / (C * C), None, ALU.mult)
        vector.wait_ge(sems["ldba"], 32)
        vector.tensor_scalar(sc1[0:1, 3:4], alphas[0:1, :], 0.5, None, ALU.mult)
        vector.drain().then_inc(sems["vcell"], 1)
        vector.wait_ge(sems["mrt"], 3)
        vector.tensor_copy(cstb[:, 0:8], ps_at[1][:, 0:8])
        vector.drain().then_inc(sems["bcv"], 1)
        for q in range(8):
            vector.wait_ge(sems["tch"], q + 1)
            o_ = (q % 2) * C
            vector.tensor_tensor(wtern[:, q * C:(q + 1) * C], wet[:, o_:o_ + C],
                                 wet[:, WET_ROWS_PAD + o_:WET_ROWS_PAD + o_ + C],
                                 ALU.add).then_inc(sems["tern"], 1)

        vector.wait_ge(sems["ldx2"], 16)
        for i in range(8, NT):
            if i == 12:
                vector.wait_ge(sems["ldxt"], 16)
            stats_tile(vector, xs[:, i * C:i * C + 512], xs[:, i * C + 512:(i + 1) * C], 1 + i, 128)
        stats_batch(vector, 9, 17, scm1[:, 9:17], scv1[:, 9:17], scq1[:, 9:17], scn1[:, 9:17])
        vector.drain().then_inc(sems["ln1s"], 1)
        vector.wait_ge(sems["rinv1"], 2)
        vector.scalar_tensor_tensor(scn1[:, 9:17], scm1[:, 9:17], -1.0,
                                    scr1[:, 9:17], ALU.mult, ALU.mult)
        vector.drain().then_inc(sems["cols1"], 1)

        def norm_tile(i):
            vector.wait_ge(sems["expd"], i + 1)
            off = (i % 2) * WROW
            wg = bass.AP(wpad, off, [[2 * WROW, 128], [128, H], [1, K]])
            vector.tensor_reduce(sumc[:].unsqueeze(2), wg, AX.X, ALU.add,
                                 opt_input=False, opt_output=False)
            vector.drain()
            vector.reciprocal(recc[:], sumc[:])
            vector.drain()
            recb = bass.AP(recc, 0, [[16, 128], [1, H], [0, K]])
            vector.tensor_tensor(wg, wg, recb, ALU.mult).then_inc(sems["wnorm"], 1)

        def evac_pair(i):
            vector.wait_ge(sems["agg"], 2 * (i + 1))
            vector.tensor_tensor(xs_t(i), ps_at[i % 2][:], xs_t(i), ALU.add).then_inc(sems["x1"], 1)
            if level >= 5:
                vector.drain()
                stats_tile(vector, xs[:, i * C:i * C + 512], xs[:, i * C + 512:(i + 1) * C], i, 128)
                if i == 7:
                    stats_batch(vector, 0, 8, scm2[:, 0:8], scv2[:, 0:8], scq2[:, 0:8], scn2[:, 0:8])
                    vector.drain().then_inc(sems["ln2s"], 1)

        if level >= 2:
            for i in range(NT):
                norm_tile(i)
            if level >= 4:
                for i in range(NT):
                    evac_pair(i)

        if level >= 5:
            stats_batch(vector, 8, 16, scm2[:, 8:16], scv2[:, 8:16], scq2[:, 8:16], scn2[:, 8:16])
            vector.drain().then_inc(sems["ln2s"], 1)
            a2b8 = bass.AP(cstb, 3, [[8, 128], [0, 8]])
            vector.wait_ge(sems["rinv2"], 1)
            vector.tensor_tensor(scr2[:, 0:8], scr2[:, 0:8], a2b8, ALU.mult)
            vector.drain()
            vector.scalar_tensor_tensor(scn2[:, 0:8], scm2[:, 0:8], -1.0, scr2[:, 0:8], ALU.mult, ALU.mult)
            vector.drain().then_inc(sems["cols2"], 1)
            vector.wait_ge(sems["rinv2"], 2)
            vector.tensor_tensor(scr2[:, 8:16], scr2[:, 8:16], a2b8, ALU.mult)
            vector.drain()
            vector.scalar_tensor_tensor(scn2[:, 8:16], scm2[:, 8:16], -1.0, scr2[:, 8:16], ALU.mult, ALU.mult)
            vector.drain().then_inc(sems["cols2"], 1)

        if level >= 6:
            for i in range(NT):
                fb = [ps_ff[0], ps_ff[1]] if i % 2 == 0 else [ps_wp[0], ps_wp[1]]
                for hf in range(2):
                    vector.wait_ge(sems["ffn"], 2 * i + hf + 1)
                    if i >= 4:
                        vector.wait_ge(sems["od"], 16 * ((i - 4) // 2 + 1))
                    vector.tensor_tensor(
                        outst[:, (i % 4) * C + hf * 512:(i % 4) * C + hf * 512 + 512],
                        fb[hf][:], xs[:, i * C + hf * 512: i * C + hf * 512 + 512],
                        ALU.add).then_inc(sems["outs"], 1)

    @block.scalar
    def _(scalar):
        for q_ in (1, 3):
            scalar.dma_start(
                bass.AP(xs, 2 * q_ * C, [[NT * C, 128], [C, 2], [1, C]]),
                bass.AP(x_d, 2 * q_ * 128 * C, [[C, 128], [128 * C, 2], [1, C]]),
            ).then_inc(xqs[q_], 16)
        scalar.dma_start(
            bass.AP(wTf, 0, [[8 * C, 128], [C, 8], [1, C]]),
            bass.AP(wT_d, 0, [[C, 128], [128 * C, 8], [1, C]]),
        ).then_inc(sems["ldw"], 16)
        scalar.dma_start(
            bass.AP(xs, 8 * C, [[NT * C, 128], [C, 4], [1, C]]),
            bass.AP(x_d, 8 * 128 * C, [[C, 128], [128 * C, 4], [1, C]]),
        ).then_inc(sems["ldx2"], 16)
        scalar.wait_ge(sems["ln1s"], 1)
        scalar.activation(scr1[:, 0:9], scv1[:, 0:9], ACT.Ln, bias=cst[:, 0:1])
        scalar.drain()
        scalar.activation(scr1[:, 0:9], scr1[:, 0:9], ACT.Exp, scale=cst[:, 1:2])
        scalar.drain().then_inc(sems["rinv1"], 1)
        scalar.wait_ge(sems["cols1"], 1)
        scalar.activation(xn1h[:], halox[:], ACT.Identity, bias=scn1[0:32, 0:1], scale=scr1[0:32, 0:1])
        scalar.drain().then_inc(sems["xn1a"], 1)
        if level >= 3:
            scalar.dma_start(bass.AP(xn1_d, 0, [[C, 32], [1, C]]),
                             xn1h[:]).then_inc(x1ds[0], 16)
        for i_e in range(4):
            scalar.activation(wins[:, (6 + i_e) * C:(7 + i_e) * C], xs_t(i_e), ACT.Identity,
                              bias=scn1[:, i_e + 1:i_e + 2],
                              scale=scr1[:, i_e + 1:i_e + 2]).then_inc(sems["xn1a"], 1)
        if level >= 2:
            scalar.drain()
            scalar.dma_start(
                bass.AP(xn1_d, 32 * C, [[C, 128], [128 * C, 4], [1, C]]),
                bass.AP(wins, 6 * C, [[10 * C, 128], [C, 4], [1, C]]),
            ).then_inc(x1ds[1], 16)
        scalar.wait_ge(sems["vcell"], 1)
        scalar.wait_ge(sems["vprep"], 1)
        scalar.activation(sc1[0:1, 2:3], sc1[0:1, 1:2], ACT.Ln)
        scalar.drain()
        scalar.activation(sc1[0:1, 2:3], sc1[0:1, 2:3], ACT.Exp, scale=cst[0:1, 2:3])
        scalar.drain()
        scalar.activation(sc1[0:1, 2:3], sc1[0:1, 2:3], ACT.Identity,
                          scale=cst[0:1, 4:5], bias=cst[0:1, 3:4])
        scalar.activation(sc1[0:1, 4:5], sc1[0:1, 0:1], ACT.Identity, scale=cst[0:1, 5:6])
        scalar.drain()

        scalar.sem_inc(sems["scell"], 1)
        scalar.wait_ge(sems["bcv"], 1)
        for q in range(8):
            if q > 1:
                scalar.wait_ge(sems["tern"], q - 1)
            o_ = (q % 2) * C
            scalar.sign(wet[:, o_:o_ + C], wTf[:, q * C:(q + 1) * C], bias=cstb[:, 4:5])
            scalar.sign(wet[:, WET_ROWS_PAD + o_:WET_ROWS_PAD + o_ + C], wTf[:, q * C:(q + 1) * C],
                        bias=cstb[:, 0:1]).then_inc(sems["tch"], 1)

        wpbanks_s = [ps_wp[0], ps_wp[1], ps_ff[0], ps_ff[1]]

        def exp_act(i):
            scalar.wait_ge(sems["wp"], i + 1)
            if i >= 2 and level >= 3:
                scalar.wait_ge(stgs[i // 2 - 1], 32)
            if i >= 2:
                scalar.wait_ge(sems["wnorm"], i - 1)
            if i < 2:
                scalar.wait_ge(sems["gz"], 1)
                if level >= 3:
                    scalar.wait_ge(sems["stl"], 48)
            off = (i % 2) * WROW
            outap = bass.AP(wpad, off, [[2 * WROW, 128], [128, H], [1, K]])
            inap = bass.AP(wpbanks_s[i % 4], 0, [[512, 128], [K, H], [1, K]])
            scalar.activation(outap, inap, ACT.Exp,
                              bias=cstb[:, 2:3]).then_inc(sems["expd"], 1)

        def win_read(p):
            if level < 4 and p >= 5:
                return
            wmax = min(3, (128 * p + 225) // 512)
            scalar.wait_ge(x1ds[0], 16)
            for w_ in range(wmax + 1):
                scalar.wait_ge(x1ds[1 + w_], 16)
            if p >= 5 and level >= 4:
                scalar.wait_ge(sems["agg"], 2 * p - 8)
            if p == 15:
                scalar.wait_ge(sems["stl"], 48)
            s = (2 * p) % 10
            scalar.dma_start(
                bass.AP(wins, s * C, [[10 * C, 128], [C, 2], [1, C]]),
                bass.AP(xn1_d, (128 * p + 2) * C, [[C, 128], [64 * C, 2], [1, C]]),
            ).then_inc(winbs[p], 16)

        def stage_wr(b):
            scalar.wait_ge(sems["wnorm"], 2 * b + 2)
            scalar.dma_start(
                bass.AP(stage_d, b * BPITCH, [[WROW, 64], [2 * P_SUB, 2], [1, WROW]]),
                bass.AP(wpad, 0, [[2 * WROW, 64], [WROW, 2], [1, WROW]]),
            ).then_inc(stgs[b], 16)
            scalar.dma_start(
                bass.AP(stage_d, b * BPITCH + P_SUB, [[WROW, 64], [2 * P_SUB, 2], [1, WROW]]),
                bass.AP(wpad, 64 * 2 * WROW, [[2 * WROW, 64], [WROW, 2], [1, WROW]]),
            ).then_inc(stgs[b], 16)

        nextp = 0
        for i in range(4, NT):
            if i == 5:
                scalar.wait_ge(sems["ln1s"], 2)
                scalar.activation(scr1[:, 9:17], scv1[:, 9:17], ACT.Ln, bias=cst[:, 0:1])
                scalar.drain()
                scalar.activation(scr1[:, 9:17], scr1[:, 9:17], ACT.Exp, scale=cst[:, 1:2])
                scalar.drain().then_inc(sems["rinv1"], 1)
            if i == 8:
                scalar.wait_ge(sems["cols1"], 2)
            scalar.activation(xn1[:, i * C:(i + 1) * C], xs_t(i), ACT.Identity,
                              bias=scn1[:, i + 1:i + 2], scale=scr1[:, i + 1:i + 2]).then_inc(sems["xn1a"], 1)
            if level >= 2 and i % 4 == 3:
                w = i // 4
                scalar.drain()
                scalar.dma_start(
                    bass.AP(xn1_d, (32 + 512 * w) * C, [[C, 128], [128 * C, 4], [1, C]]),
                    bass.AP(xn1, 4 * w * C, [[NT * C, 128], [C, 4], [1, C]]),
                ).then_inc(x1ds[1 + w], 16)
            if level >= 2:
                j = i - 4
                exp_act(j)
                if level >= 3 and j % 2 == 1:
                    stage_wr(j // 2)
            if level >= 2 and i % 4 == 3:
                w = i // 4
                while nextp < 5 and 128 * nextp + 258 <= 32 + 512 * (w + 1):
                    win_read(nextp)
                    nextp += 1
        if level >= 2:
            exp_act(12)
            exp_act(13)
            if level >= 3:
                stage_wr(6)
            exp_act(14)
            exp_act(15)
            if level >= 3:
                stage_wr(7)
        if level >= 3:
            while nextp < 16:
                win_read(nextp)
                nextp += 1

        if level >= 5:
            scalar.wait_ge(sems["ln2s"], 1)
            scalar.activation(scr2[:, 0:8], scv2[:, 0:8], ACT.Ln, bias=cst[:, 0:1])
            scalar.drain()
            scalar.activation(scr2[:, 0:8], scr2[:, 0:8], ACT.Exp, scale=cst[:, 1:2])
            scalar.drain().then_inc(sems["rinv2"], 1)
            scalar.wait_ge(sems["cols2"], 1)
            for i in range(NT):
                if i == 8:
                    scalar.wait_ge(sems["ln2s"], 2)
                    scalar.activation(scr2[:, 8:16], scv2[:, 8:16], ACT.Ln, bias=cst[:, 0:1])
                    scalar.drain()
                    scalar.activation(scr2[:, 8:16], scr2[:, 8:16], ACT.Exp, scale=cst[:, 1:2])
                    scalar.drain().then_inc(sems["rinv2"], 1)
                    scalar.wait_ge(sems["cols2"], 2)
                if i >= 4:
                    scalar.wait_ge(x2ws[i // 4 - 1], 16)
                scalar.activation(xn2r[:, (i % 4) * C:(i % 4 + 1) * C], xs_t(i), ACT.Identity,
                                  bias=scn2[:, i:i + 1], scale=scr2[:, i:i + 1]).then_inc(sems["xn2a"], 1)
                if i % 4 == 3:
                    w_ = i // 4
                    scalar.drain()
                    scalar.dma_start(
                        bass.AP(xn2_d, 4 * w_ * 128 * C, [[C, 128], [128 * C, 4], [1, C]]),
                        bass.AP(xn2r, 0, [[4 * C, 128], [C, 4], [1, C]]),
                    ).then_inc(x2ws[w_], 16)
            if level >= 6:
                for w in range(8):
                    scalar.wait_ge(sems["outs"], 4 * (w + 1))
                    if w > 0:
                        scalar.wait_ge(sems["od"], 16 * w)
                    scalar.dma_start(
                        bass.AP(out_d, 2 * w * 128 * C, [[C, 128], [128 * C, 2], [1, C]]),
                        bass.AP(outst, ((2 * w) % 4) * C, [[4 * C, 128], [C, 2], [1, C]]),
                    ).then_inc(sems["od"], 16)


    @block.tensor
    def _(tensor):
        tensor.wait_ge(sems["ldwl"], 16)
        tensor.wait_ge(sems["vprep"], 1)
        tensor.wait_ge(sems["vsq"], 1)
        for cc in range(8):
            mm = tensor.matmul(ps_wp[1][0:1, 0:F], onesbc[:], wet[:, cc * F:(cc + 1) * F],
                               start=(cc == 0), stop=(cc == 7))
        mm.then_inc(sems["mrt"], 1)
        tensor.wait_ge(sems["vabs"], 1)
        tensor.matmul(ps_at[0][0:1, 0:1], onesfc[:], redt[:, 8:9],
                      start=True, stop=True).then_inc(sems["mrt"], 1)
        tensor.wait_ge(sems["scell"], 1)
        tensor.matmul(ps_at[1][:, 0:8], onesfr[:], sc1[0:1, :],
                      start=True, stop=True).then_inc(sems["mrt"], 1)

        wpbanks = [ps_wp[0], ps_wp[1], ps_ff[0], ps_ff[1]]

        def wproj_tile(i):
            if i % 4 == 0:
                tensor.wait_ge(x1ts[i // 4], 128)
            if i >= 4:
                tensor.wait_ge(sems["expd"], i - 3)
            pw = wpbanks[i % 4]
            for cc in range(8):
                mm = tensor.matmul(pw[:, 0:F], xn1T[:, cc * T + i * 128:cc * T + (i + 1) * 128],
                                   wlTb[:, cc * F:(cc + 1) * F], start=(cc == 0), stop=(cc == 7))
            mm.then_inc(sems["wp"], 1)

        def agg_pair(i):
            b = i // 2
            tensor.wait_ge(wetbs[b], 16)
            tensor.wait_ge(winbs[i], 16)
            if i == 0:
                tensor.wait_ge(sems["vcell"], 1)
            if i == 1:
                tensor.wait_ge(sems["bcv"], 1)
            if i >= 2:
                tensor.wait_ge(sems["x1"], i - 1)
            pa = ps_at[i % 2]
            for jj in range(2):
                j = 2 * i + jj
                bb = b % 3
                colbase = (j - 4 * b) * 1025
                s = (2 * i) % 10 + jj
                for h in range(H):
                    if bb == 2:
                        lhsT = bass.AP(wet3, colbase + h, [[WET_ROWS_PAD, 128], [16, 64]])
                    else:
                        lhsT = bass.AP(wet, bb * WET_ROWS_PAD + colbase + h,
                                       [[2 * WET_ROWS_PAD, 128], [16, 64]])
                    rhs = wins[:, s * C + h * R:s * C + (h + 1) * R]
                    mm = tensor.matmul(pa[64 * jj:64 * jj + 64, h * R:(h + 1) * R], lhsT, rhs,
                                       start=True, stop=True)
                mm.then_inc(sems["agg"], 1)

        if level >= 2:
            tensor.wait_ge(sems["vcell"], 1)
            for i in range(NT):
                wproj_tile(i)
            if level >= 4:
                for i in range(NT):
                    agg_pair(i)

        if level >= 6:
            tensor.wait_ge(sems["tern"], 8)
            tensor.wait_ge(sems["ldba"], 32)
            for i in range(NT):
                if i % 4 == 0:
                    tensor.wait_ge(xnts[i // 4], 128)
                if i >= 2:
                    tensor.wait_ge(sems["outs"], 2 * (i - 2) + 2)
                fb = [ps_ff[0], ps_ff[1]] if i % 2 == 0 else [ps_wp[0], ps_wp[1]]
                for cc in range(8):
                    lhs = xn2T[:, cc * T + i * 128:cc * T + (i + 1) * 128]
                    tensor.matmul(fb[0][:], lhs, wtern[:, cc * C:cc * C + 512],
                                  start=(cc == 0), stop=False)
                    tensor.matmul(fb[1][:], lhs, wtern[:, cc * C + 512:cc * C + 1024],
                                  start=(cc == 0), stop=False)
                tensor.matmul(fb[0][:], onesbr[0:1, :], biasb[0:1, 0:512],
                              start=False, stop=True).then_inc(sems["ffn"], 1)
                tensor.matmul(fb[1][:], onesbr[0:1, :], biasb[0:1, 512:1024],
                              start=False, stop=True).then_inc(sems["ffn"], 1)

    stack.close()
    return nc


def make_inputs(x, wl, wf, al_, bi):
    x = np.asarray(x, dtype=np.float32)
    wlT = np.ascontiguousarray(np.asarray(wl, dtype=np.float32).T)
    wT = np.ascontiguousarray(np.asarray(wf, dtype=np.float32).T)
    al_ = np.asarray(al_, dtype=np.float32).reshape(1, 1)
    bi = np.asarray(bi, dtype=np.float32).reshape(1, C)
    in_maps = []
    for c in range(NCORES):
        seq, half = c // 2, c % 2
        xs_ = np.ascontiguousarray(x[seq, half * T:(half + 1) * T])
        if half == 0:
            haloa = np.zeros((32, C), dtype=np.float32)
        else:
            haloa = np.ascontiguousarray(x[seq, half * T - 32:half * T])
        in_maps.append({"x": xs_, "halo": haloa, "wlT": wlT, "wT": wT,
                        "bias": bi, "alpha": al_})
    return in_maps


def kernel(L0_attn_x, L0_attn_w_lin, L0_ffn_weight, L0_ffn_alpha, L0_ffn_bias):
    in_maps = make_inputs(L0_attn_x, L0_attn_w_lin, L0_ffn_weight, L0_ffn_alpha, L0_ffn_bias)
    nc = build_nc(level=LVL_FULL)
    res = run_bass_kernel_spmd(nc, in_maps, core_ids=list(range(NCORES)))
    out = np.empty((4, 4096, C), dtype=np.float32)
    for c in range(NCORES):
        seq, half = c // 2, c % 2
        out[seq, half * T:(half + 1) * T] = res.results[c]["out"]
    return out


# version 22
# speedup vs baseline: 1.0179x; 1.0023x over previous
import numpy as np
import concourse.bass as bass
import concourse.mybir as mybir
from concourse.bass_utils import run_bass_kernel_spmd

F32 = mybir.dt.float32
BF16 = mybir.dt.bfloat16
ALU = mybir.AluOpType
ACT = mybir.ActivationFunctionType
AX = mybir.AxisListType

NCORES = 8
T = 2048
C = 1024
H = 16
K = 31
R = 64
F = H * K
NT = 16
NSUB = 32
EPS = 1e-5

WROW = 16 * 128 + 1
P_SUB = 128 * 1025
WET_BATCH = 4
WET_ROWS_PAD = 4112
N_WB = NSUB // WET_BATCH
BPITCH = 4 * P_SUB + 1536

LVL_FULL = 6


def build_nc(level=LVL_FULL, tap=None):
    nc = bass.Bass("TRN2", target_bir_lowering=False)

    x_d = nc.dram_tensor("x", [T, C], F32, kind="ExternalInput")
    halo_d = nc.dram_tensor("halo", [32, C], F32, kind="ExternalInput")
    wlT_d = nc.dram_tensor("wlT", [C, F], F32, kind="ExternalInput")
    wT_d = nc.dram_tensor("wT", [C, C], F32, kind="ExternalInput")
    bias_d = nc.dram_tensor("bias", [1, C], F32, kind="ExternalInput")
    alpha_d = nc.dram_tensor("alpha", [1, 1], F32, kind="ExternalInput")
    out_d = nc.dram_tensor("out", [T, C], F32, kind="ExternalOutput")
    stage_d = nc.dram_tensor("stage", [N_WB * BPITCH + 65536], BF16, kind="Internal")
    xn1_d = nc.dram_tensor("xn1d", [2144 * C], BF16, kind="Internal")
    xn2_d = nc.dram_tensor("xn2d", [T, C], BF16, kind="Internal")

    TAPS = {
        "xn1":  ([128, NT * C], BF16),
        "wpad": ([128, 2 * WROW], BF16),
        "wet":  ([128, 2 * WET_ROWS_PAD], BF16),
        "wins": ([128, 10 * C], BF16),
        "x1":   ([128, NT * C], F32),
        "xn2T": ([128, 8 * T], BF16),
        "scm1": ([128, 17], F32),
        "scv1": ([128, 17], F32),
        "scr1": ([128, 17], F32),
        "scn1": ([128, 17], F32),
        "stats": ([128, 12], F32),
    }
    dbg_d = None
    if tap is not None:
        dbg_d = nc.dram_tensor("dbg", TAPS[tap][0], TAPS[tap][1], kind="ExternalOutput")

    al = nc.alloc_sbuf_tensor
    xs = al("xs", [128, NT * C], F32)
    wtern = al("wtern", [128, 8 * C], BF16)
    wet = al("wet", [128, 2 * WET_ROWS_PAD], BF16, is_transpose_dest=True)
    wins = al("wins", [128, 10 * C], BF16)
    wpad = al("wpad", [128, 2 * WROW], BF16)
    wlTb = al("wlTb", [128, 8 * F], BF16)
    halox = al("halox", [32, C], F32)
    xn1h = al("xn1h", [32, C], BF16)
    onesbc = al("onesbc", [128, 1], BF16)
    onesbr = al("onesbr", [1, 128], BF16)
    onesfr = al("onesfr", [1, 128], F32)
    onesfc = al("onesfc", [128, 1], F32)
    biasb = al("biasb", [1, C], BF16)
    alphas = al("alphas", [1, 1], F32)
    statsw = al("statsw", [128, 17 * 12], F32)
    mq = al("mq", [128, 2 * 68], F32)
    scm1 = al("scm1", [128, 17], F32)
    scv1 = al("scv1", [128, 17], F32)
    scr1 = al("scr1", [128, 17], F32)
    scn1 = al("scn1", [128, 17], F32)
    scm2 = al("scm2", [128, 16], F32)
    scv2 = al("scv2", [128, 16], F32)
    scr2 = al("scr2", [128, 16], F32)
    scn2 = al("scn2", [128, 16], F32)
    scq1 = al("scq1", [128, 17], F32)
    scq2 = al("scq2", [128, 16], F32)
    sumc = al("sumc", [128, 16], F32)
    recc = al("recc", [128, 16], F32)
    cst = al("cst", [128, 6], F32)
    cstb = al("cstb", [128, 8], F32)
    sc1 = al("sc1", [1, 8], F32)
    redt = al("redt", [128, 16], F32)

    ra = nc.sbuf_top - 32768
    wTf = nc.alloc_sbuf_tensor_at("wTf", [128, 8 * C], F32, offset=ra)
    xn1 = nc.alloc_sbuf_tensor_at("xn1", [128, NT * C], BF16, offset=ra)
    xn2T = nc.alloc_sbuf_tensor_at("xn2T", [128, 8 * T], BF16, offset=ra)
    rb = ra - 32768
    xn1T = nc.alloc_sbuf_tensor_at("xn1T", [128, 8 * T], BF16, offset=rb)
    wet3 = nc.alloc_sbuf_tensor_at("wet3", [128, WET_ROWS_PAD], BF16, offset=ra)
    outst = nc.alloc_sbuf_tensor_at("outst", [128, 4 * C], F32, offset=rb)
    xn2r = nc.alloc_sbuf_tensor_at("xn2r", [128, 4 * C], BF16, offset=rb + 16384)

    ps_wp = [nc.alloc_psum_tensor(f"ps_wp{i}", [128, 512], F32) for i in range(2)]
    ps_at = [nc.alloc_psum_tensor(f"ps_at{i}", [128, 1024], F32) for i in range(2)]
    ps_ff = [nc.alloc_psum_tensor(f"ps_ff{i}", [128, 512], F32) for i in range(2)]

    import contextlib
    stack = contextlib.ExitStack()
    sems = {}
    for s in ["lds", "ldx", "ldx2", "ldh", "ldw", "ldxt", "ldwl", "ldba", "x1d", "vprep", "gz", "vsq", "vabs", "mrt", "mrv", "mrf", "vcell",
              "scell", "bcv", "tch", "tern", "ln1s", "rinv1", "cols1", "xn1a", "wp",
              "expd", "wnorm", "stg", "stl", "winb", "wetb", "agg", "x1", "ln2s",
              "rinv2", "cols2", "xn2a", "x2w", "xnt", "ffn", "outs", "od", "dbgs"]:
        sems[s] = stack.enter_context(nc.semaphore(s))
    xqs = [stack.enter_context(nc.semaphore(f"xq{i}")) for i in range(4)]
    winbs = [stack.enter_context(nc.semaphore(f"winb{i}")) for i in range(16)]
    x1ts = [stack.enter_context(nc.semaphore(f"x1t{i}")) for i in range(4)]
    xnts = [stack.enter_context(nc.semaphore(f"xnt{i}")) for i in range(4)]
    stgs = [stack.enter_context(nc.semaphore(f"stg{i}")) for i in range(8)]
    wetbs = [stack.enter_context(nc.semaphore(f"wetb{i}")) for i in range(8)]
    x2ws = [stack.enter_context(nc.semaphore(f"x2w{i}")) for i in range(4)]
    x1ds = [stack.enter_context(nc.semaphore(f"x1d{i}")) for i in range(5)]
    ods = [sems[k] for k in ("lds", "ldx", "x1d", "winb", "wetb", "stg", "x2w", "mrv")]

    nc._sems = sems
    block = stack.enter_context(nc.Block())

    def xs_t(i):
        return xs[:, i * C:(i + 1) * C]

    def stats_tile(vector, src_half1, src_half2, slot, np_):
        p = slice(0, np_)
        vector.bn_stats(statsw[p, slot * 12: slot * 12 + 6], src_half1)
        vector.bn_stats(statsw[p, slot * 12 + 6: slot * 12 + 12], src_half2)

    def stats_batch(vector, s0, s1, scm, scv, scq, scn):
        n = s1 - s0
        vector.drain()
        m_src4 = bass.AP(statsw, s0 * 12 + 1, [[204, 128], [12, n], [3, 4], [1, 1]])
        q_src3 = bass.AP(statsw, s0 * 12 + 2, [[204, 128], [12, n], [3, 4]])
        mc = bass.AP(mq, 0, [[136, 128], [4, n], [1, 4]])
        mc_flat = bass.AP(mq, 0, [[136, 128], [1, 4 * n]])
        vector.tensor_reduce(mc, m_src4, AX.X, ALU.add, opt_input=False, opt_output=False)
        vector.tensor_reduce(scv.unsqueeze(2), q_src3, AX.X, ALU.add, opt_input=False, opt_output=False)
        vector.drain()
        vector.tensor_reduce(scm.unsqueeze(2), mc, AX.X, ALU.add, opt_input=False, opt_output=False)
        vector.drain()
        vector.tensor_tensor(mc_flat, mc_flat, mc_flat, ALU.mult)
        vector.drain()
        vector.tensor_reduce(scq.unsqueeze(2), mc, AX.X, ALU.add, opt_input=False, opt_output=False)
        vector.tensor_scalar(scm, scm, 0.25, None, ALU.mult)
        vector.tensor_scalar(scv, scv, 1.0 / C, None, ALU.mult)
        vector.drain()
        vector.tensor_scalar(scq, scq, 0.25, None, ALU.mult)
        vector.tensor_tensor(scn, scm, scm, ALU.mult)
        vector.drain()
        vector.tensor_tensor(scv, scv, scq, ALU.add)
        vector.drain()
        vector.tensor_tensor(scv, scv, scn, ALU.subtract)
        vector.drain()

    @block.sync
    def _(sync):
        sync.dma_start(halox[:], halo_d[:]).then_inc(sems["ldh"], 16)
        for q_ in (0, 2):
            sync.dma_start(
                bass.AP(xs, 2 * q_ * C, [[NT * C, 128], [C, 2], [1, C]]),
                bass.AP(x_d, 2 * q_ * 128 * C, [[C, 128], [128 * C, 2], [1, C]]),
            ).then_inc(xqs[q_], 16)

        if level >= 3:
            sync.wait_ge(sems["gz"], 1)
            sync.dma_start(bass.AP(xn1_d, 2080 * C, [[C, 64], [1, C]]),
                           wpad[0:64, 0:C]).then_inc(sems["stl"], 16)
            sync.dma_start(bass.AP(stage_d, 131136, [[BPITCH, 8], [P_SUB, 4], [1, 64]]),
                           bass.AP(wpad, 0, [[2 * WROW, 8], [64, 4], [1, 64]])).then_inc(sems["stl"], 16)
            sync.dma_start(bass.AP(stage_d, 4 * P_SUB, [[BPITCH, 8], [1, 1536]]),
                           wpad[0:8, 0:1536]).then_inc(sems["stl"], 16)

        def wet_xbar(b):
            sync.wait_ge(stgs[b], 32)
            if b <= 1:
                sync.wait_ge(sems["tern"], 8)
            if b == 2:
                sync.wait_ge(sems["tern"], 8)
                sync.wait_ge(x1ds[2], 16)
            if b >= 3 and level >= 4:
                sync.wait_ge(sems["agg"], 4 * (b - 2))
                sync.wait_ge(wetbs[b - 3], 16)
            src = bass.AP(stage_d, b * BPITCH, [[128, WET_ROWS_PAD], [1, 128]])
            if b % 3 == 2:
                dst = wet3[:, 0:WET_ROWS_PAD]
            else:
                dst = wet[:, (b % 3) * WET_ROWS_PAD:(b % 3 + 1) * WET_ROWS_PAD]
            sync.dma_start(dst, src, transpose=True).then_inc(wetbs[b], 16)

        if level >= 2:
            n_wb = (N_WB if level >= 4 else 1) if level >= 3 else 0
            for q_ in range(4):
                sync.wait_ge(x1ds[1 + q_], 16)
                for cc in range(8):
                    src_ = bass.AP(xn1_d, (32 + 512 * q_) * C + 128 * cc, [[C, 512], [1, 128]])
                    sync.dma_start(xn1T[:, cc * T + 512 * q_: cc * T + 512 * (q_ + 1)],
                                   src_, transpose=True).then_inc(x1ts[q_], 16)
                if q_ < 3 and q_ < n_wb:
                    wet_xbar(q_)
            for b in range(3 if n_wb > 3 else n_wb, n_wb):
                wet_xbar(b)

        if level >= 5:
            for w in range(4):
                if w == 0 and level >= 4:
                    sync.wait_ge(sems["agg"], 24)
                sync.wait_ge(x2ws[w], 16)
                for cc in range(8):
                    src = bass.AP(xn2_d, 512 * w * C + 128 * cc, [[C, 512], [1, 128]])
                    sync.dma_start(xn2T[:, cc * T + 512 * w: cc * T + 512 * (w + 1)],
                                   src, transpose=True).then_inc(xnts[w], 16)



    @block.gpsimd
    def _(gpsimd):
        gpsimd.memset(wpad[:], 0.0)
        gpsimd.drain().then_inc(sems["gz"], 1)
        gpsimd.dma_start(
            bass.AP(wlTb, 0, [[8 * F, 128], [F, 8], [1, F]]),
            bass.AP(wlT_d, 0, [[F, 128], [128 * F, 8], [1, F]]),
        ).then_inc(sems["ldwl"], 16)
        gpsimd.dma_start(biasb[:], bias_d[:]).then_inc(sems["ldba"], 16)
        gpsimd.dma_start(alphas[:], alpha_d[:]).then_inc(sems["ldba"], 16)
        gpsimd.dma_start(
            bass.AP(xs, 12 * C, [[NT * C, 128], [C, 4], [1, C]]),
            bass.AP(x_d, 12 * 128 * C, [[C, 128], [128 * C, 4], [1, C]]),
        ).then_inc(sems["ldxt"], 16)

        finals = [("ldx2", 16), ("ldxt", 16), ("ldh", 16), ("ldw", 16), ("ldwl", 16), ("ldba", 32)]
        fin = {1: (sems["xn1a"], 17), 2: (sems["wnorm"], 16), 3: (wetbs[1], 16),
               4: (sems["x1"], 16), 5: (xnts[3], 128), 6: (ods[7], 16)}[level]
        gpsimd.wait_ge(fin[0], fin[1])
        for nm, v in finals:
            gpsimd.wait_ge(sems[nm], v)
        for s_ in xqs:
            gpsimd.wait_ge(s_, 16)
        if level >= 2:
            for s_ in x1ts:
                gpsimd.wait_ge(s_, 128)
            for s_ in x1ds[1:]:
                gpsimd.wait_ge(s_, 16)
        if level >= 3:
            gpsimd.wait_ge(sems["stl"], 48)
            gpsimd.wait_ge(x1ds[0], 16)
            nwin = 4 if level < 4 else 16
            for p_ in range(nwin):
                gpsimd.wait_ge(winbs[p_], 16)
            for b_ in range(8):
                gpsimd.wait_ge(stgs[b_], 32)
            for b_ in range(2 if level < 4 else 8):
                gpsimd.wait_ge(wetbs[b_], 16)
        if level >= 5:
            for s_ in x2ws:
                gpsimd.wait_ge(s_, 16)
            for s_ in xnts:
                gpsimd.wait_ge(s_, 128)
        if level >= 6:
            for s_ in ods:
                gpsimd.wait_ge(s_, 16)
        if tap is not None:
            src = {"xn1": xn1, "wpad": wpad, "wet": wet, "wins": wins,
                   "x1": xs, "xn2T": xn2T, "scm1": scm1, "scv1": scv1,
                   "scr1": scr1, "scn1": scn1, "stats": stats}[tap]
            gpsimd.dma_start(dbg_d.ap(), src[:]).then_inc(sems["dbgs"], 16)
            gpsimd.wait_ge(sems["dbgs"], 16)

    @block.vector
    def _(vector):
        vector.memset(cst[:, 0:1], EPS)
        vector.memset(cst[:, 1:2], -0.5)
        vector.memset(cst[:, 2:3], 0.5)
        vector.memset(cst[:, 3:4], -2.0)
        vector.memset(cst[:, 4:5], -33.0)
        vector.memset(cst[:, 5:6], -1.0)
        vector.memset(sc1[:], 0.0)
        vector.memset(scm1[:, 0:1], 0.0)
        vector.memset(scv1[:, 0:1], 1.0)
        vector.memset(scq1[:, 0:1], 0.0)
        vector.memset(scn1[:, 0:1], 0.0)
        vector.memset(onesbc[:], 1.0)
        vector.memset(onesbr[:], 1.0)
        vector.memset(onesfr[:], 1.0)
        vector.memset(onesfc[:], 1.0)
        vector.drain().then_inc(sems["vprep"], 1)
        vector.wait_ge(sems["ldh"], 16)
        vector.memset(statsw[:, 0:12], 0.0)
        vector.drain()
        stats_tile(vector, halox[:, 0:512], halox[:, 512:C], 0, 32)
        for i in range(8):
            if i % 2 == 0:
                vector.wait_ge(xqs[i // 2], 16)
            stats_tile(vector, xs[:, i * C:i * C + 512], xs[:, i * C + 512:(i + 1) * C], 1 + i, 128)
        stats_batch(vector, 0, 9, scm1[:, 0:9], scv1[:, 0:9], scq1[:, 0:9], scn1[:, 0:9])
        vector.drain().then_inc(sems["ln1s"], 1)
        vector.wait_ge(sems["rinv1"], 1)
        vector.scalar_tensor_tensor(scn1[:, 0:9], scm1[:, 0:9], -1.0, scr1[:, 0:9], ALU.mult, ALU.mult)
        vector.drain().then_inc(sems["cols1"], 1)
        vector.wait_ge(sems["ldwl"], 16)
        vector.tensor_tensor(wet[:, 0:8 * F], wlTb[:], wlTb[:], ALU.mult)
        vector.drain().then_inc(sems["vsq"], 1)
        vector.wait_ge(sems["ldw"], 16)
        for q in range(8):
            vector.tensor_reduce(redt[:, q:q + 1].unsqueeze(2), wTf[:, q * C:(q + 1) * C].unsqueeze(1),
                                 AX.X, ALU.add, apply_absolute_value=True)
        vector.drain()
        vector.tensor_reduce(redt[:, 8:9].unsqueeze(2), redt[:, 0:8].unsqueeze(1), AX.X, ALU.add)
        vector.drain().then_inc(sems["vabs"], 1)
        vector.wait_ge(sems["mrt"], 1)
        vector.tensor_reduce(sc1[0:1, 1:2].unsqueeze(2), ps_wp[1][0:1, 0:F].unsqueeze(1), AX.X, ALU.max)
        vector.wait_ge(sems["mrt"], 2)
        vector.tensor_scalar(sc1[0:1, 0:1], ps_at[0][0:1, 0:1], 0.7 / (C * C), None, ALU.mult)
        vector.wait_ge(sems["ldba"], 32)
        vector.tensor_scalar(sc1[0:1, 3:4], alphas[0:1, :], 0.5, None, ALU.mult)
        vector.drain().then_inc(sems["vcell"], 1)
        vector.wait_ge(sems["mrt"], 3)
        vector.tensor_copy(cstb[:, 0:8], ps_at[1][:, 0:8])
        vector.drain().then_inc(sems["bcv"], 1)
        for q in range(8):
            vector.wait_ge(sems["tch"], q + 1)
            o_ = (q % 2) * C
            vector.tensor_tensor(wtern[:, q * C:(q + 1) * C], wet[:, o_:o_ + C],
                                 wet[:, WET_ROWS_PAD + o_:WET_ROWS_PAD + o_ + C],
                                 ALU.add).then_inc(sems["tern"], 1)

        vector.wait_ge(sems["ldx2"], 16)
        for i in range(8, NT):
            if i == 12:
                vector.wait_ge(sems["ldxt"], 16)
            stats_tile(vector, xs[:, i * C:i * C + 512], xs[:, i * C + 512:(i + 1) * C], 1 + i, 128)
        stats_batch(vector, 9, 17, scm1[:, 9:17], scv1[:, 9:17], scq1[:, 9:17], scn1[:, 9:17])
        vector.drain().then_inc(sems["ln1s"], 1)
        vector.wait_ge(sems["rinv1"], 2)
        vector.scalar_tensor_tensor(scn1[:, 9:17], scm1[:, 9:17], -1.0,
                                    scr1[:, 9:17], ALU.mult, ALU.mult)
        vector.drain().then_inc(sems["cols1"], 1)

        def norm_tile(i):
            vector.wait_ge(sems["expd"], i + 1)
            off = (i % 2) * WROW
            wg = bass.AP(wpad, off, [[2 * WROW, 128], [128, H], [1, K]])
            vector.tensor_reduce(sumc[:].unsqueeze(2), wg, AX.X, ALU.add,
                                 opt_input=False, opt_output=False)
            vector.drain()
            vector.reciprocal(recc[:], sumc[:])
            vector.drain()
            recb = bass.AP(recc, 0, [[16, 128], [1, H], [0, K]])
            vector.tensor_tensor(wg, wg, recb, ALU.mult).then_inc(sems["wnorm"], 1)

        def evac_pair(i):
            vector.wait_ge(sems["agg"], 2 * (i + 1))
            vector.tensor_tensor(xs_t(i), ps_at[i % 2][:], xs_t(i), ALU.add).then_inc(sems["x1"], 1)
            if level >= 5:
                vector.drain()
                stats_tile(vector, xs[:, i * C:i * C + 512], xs[:, i * C + 512:(i + 1) * C], i, 128)
                if i == 7:
                    stats_batch(vector, 0, 8, scm2[:, 0:8], scv2[:, 0:8], scq2[:, 0:8], scn2[:, 0:8])
                    vector.drain().then_inc(sems["ln2s"], 1)

        if level >= 2:
            for i in range(NT):
                norm_tile(i)
            if level >= 4:
                for i in range(NT):
                    evac_pair(i)

        if level >= 5:
            stats_batch(vector, 8, 16, scm2[:, 8:16], scv2[:, 8:16], scq2[:, 8:16], scn2[:, 8:16])
            vector.drain().then_inc(sems["ln2s"], 1)
            a2b8 = bass.AP(cstb, 3, [[8, 128], [0, 8]])
            vector.wait_ge(sems["rinv2"], 1)
            vector.tensor_tensor(scr2[:, 0:8], scr2[:, 0:8], a2b8, ALU.mult)
            vector.drain()
            vector.scalar_tensor_tensor(scn2[:, 0:8], scm2[:, 0:8], -1.0, scr2[:, 0:8], ALU.mult, ALU.mult)
            vector.drain().then_inc(sems["cols2"], 1)
            vector.wait_ge(sems["rinv2"], 2)
            vector.tensor_tensor(scr2[:, 8:16], scr2[:, 8:16], a2b8, ALU.mult)
            vector.drain()
            vector.scalar_tensor_tensor(scn2[:, 8:16], scm2[:, 8:16], -1.0, scr2[:, 8:16], ALU.mult, ALU.mult)
            vector.drain().then_inc(sems["cols2"], 1)

        if level >= 6:
            for i in range(NT):
                fb = [ps_ff[0], ps_ff[1]] if i % 2 == 0 else [ps_wp[0], ps_wp[1]]
                for hf in range(2):
                    vector.wait_ge(sems["ffn"], 2 * i + hf + 1)
                    if i >= 4:
                        vector.wait_ge(ods[(i - 4) // 2], 16)
                    vector.tensor_tensor(
                        outst[:, (i % 4) * C + hf * 512:(i % 4) * C + hf * 512 + 512],
                        fb[hf][:], xs[:, i * C + hf * 512: i * C + hf * 512 + 512],
                        ALU.add).then_inc(sems["outs"], 1)

    @block.scalar
    def _(scalar):
        for q_ in (1, 3):
            scalar.dma_start(
                bass.AP(xs, 2 * q_ * C, [[NT * C, 128], [C, 2], [1, C]]),
                bass.AP(x_d, 2 * q_ * 128 * C, [[C, 128], [128 * C, 2], [1, C]]),
            ).then_inc(xqs[q_], 16)
        scalar.dma_start(
            bass.AP(wTf, 0, [[8 * C, 128], [C, 8], [1, C]]),
            bass.AP(wT_d, 0, [[C, 128], [128 * C, 8], [1, C]]),
        ).then_inc(sems["ldw"], 16)
        scalar.dma_start(
            bass.AP(xs, 8 * C, [[NT * C, 128], [C, 4], [1, C]]),
            bass.AP(x_d, 8 * 128 * C, [[C, 128], [128 * C, 4], [1, C]]),
        ).then_inc(sems["ldx2"], 16)
        scalar.wait_ge(sems["ln1s"], 1)
        scalar.activation(scr1[:, 0:9], scv1[:, 0:9], ACT.Ln, bias=cst[:, 0:1])
        scalar.drain()
        scalar.activation(scr1[:, 0:9], scr1[:, 0:9], ACT.Exp, scale=cst[:, 1:2])
        scalar.drain().then_inc(sems["rinv1"], 1)
        scalar.wait_ge(sems["cols1"], 1)
        scalar.activation(xn1h[:], halox[:], ACT.Identity, bias=scn1[0:32, 0:1], scale=scr1[0:32, 0:1])
        scalar.drain().then_inc(sems["xn1a"], 1)
        if level >= 3:
            scalar.dma_start(bass.AP(xn1_d, 0, [[C, 32], [1, C]]),
                             xn1h[:]).then_inc(x1ds[0], 16)
        for i_e in range(4):
            scalar.activation(wins[:, (6 + i_e) * C:(7 + i_e) * C], xs_t(i_e), ACT.Identity,
                              bias=scn1[:, i_e + 1:i_e + 2],
                              scale=scr1[:, i_e + 1:i_e + 2]).then_inc(sems["xn1a"], 1)
        if level >= 2:
            scalar.drain()
            scalar.dma_start(
                bass.AP(xn1_d, 32 * C, [[C, 128], [128 * C, 4], [1, C]]),
                bass.AP(wins, 6 * C, [[10 * C, 128], [C, 4], [1, C]]),
            ).then_inc(x1ds[1], 16)
        scalar.wait_ge(sems["vcell"], 1)
        scalar.wait_ge(sems["vprep"], 1)
        scalar.activation(sc1[0:1, 2:3], sc1[0:1, 1:2], ACT.Ln)
        scalar.drain()
        scalar.activation(sc1[0:1, 2:3], sc1[0:1, 2:3], ACT.Exp, scale=cst[0:1, 2:3])
        scalar.drain()
        scalar.activation(sc1[0:1, 2:3], sc1[0:1, 2:3], ACT.Identity,
                          scale=cst[0:1, 4:5], bias=cst[0:1, 3:4])
        scalar.activation(sc1[0:1, 4:5], sc1[0:1, 0:1], ACT.Identity, scale=cst[0:1, 5:6])
        scalar.drain()

        scalar.sem_inc(sems["scell"], 1)
        scalar.wait_ge(sems["bcv"], 1)
        for q in range(8):
            if q > 1:
                scalar.wait_ge(sems["tern"], q - 1)
            o_ = (q % 2) * C
            scalar.sign(wet[:, o_:o_ + C], wTf[:, q * C:(q + 1) * C], bias=cstb[:, 4:5])
            scalar.sign(wet[:, WET_ROWS_PAD + o_:WET_ROWS_PAD + o_ + C], wTf[:, q * C:(q + 1) * C],
                        bias=cstb[:, 0:1]).then_inc(sems["tch"], 1)

        wpbanks_s = [ps_wp[0], ps_wp[1], ps_ff[0], ps_ff[1]]

        def exp_act(i):
            scalar.wait_ge(sems["wp"], i + 1)
            if i >= 2 and level >= 3:
                scalar.wait_ge(stgs[i // 2 - 1], 32)
            if i >= 2:
                scalar.wait_ge(sems["wnorm"], i - 1)
            if i < 2:
                scalar.wait_ge(sems["gz"], 1)
                if level >= 3:
                    scalar.wait_ge(sems["stl"], 48)
            off = (i % 2) * WROW
            outap = bass.AP(wpad, off, [[2 * WROW, 128], [128, H], [1, K]])
            inap = bass.AP(wpbanks_s[i % 4], 0, [[512, 128], [K, H], [1, K]])
            scalar.activation(outap, inap, ACT.Exp,
                              bias=cstb[:, 2:3]).then_inc(sems["expd"], 1)

        def win_read(p):
            if level < 4 and p >= 5:
                return
            wmax = min(3, (128 * p + 225) // 512)
            scalar.wait_ge(x1ds[0], 16)
            for w_ in range(wmax + 1):
                scalar.wait_ge(x1ds[1 + w_], 16)
            if p >= 5 and level >= 4:
                scalar.wait_ge(sems["agg"], 2 * p - 8)
            if p == 15:
                scalar.wait_ge(sems["stl"], 48)
            s = (2 * p) % 10
            scalar.dma_start(
                bass.AP(wins, s * C, [[10 * C, 128], [C, 2], [1, C]]),
                bass.AP(xn1_d, (128 * p + 2) * C, [[C, 128], [64 * C, 2], [1, C]]),
            ).then_inc(winbs[p], 16)

        def stage_wr(b):
            scalar.wait_ge(sems["wnorm"], 2 * b + 2)
            scalar.dma_start(
                bass.AP(stage_d, b * BPITCH, [[WROW, 64], [2 * P_SUB, 2], [1, WROW]]),
                bass.AP(wpad, 0, [[2 * WROW, 64], [WROW, 2], [1, WROW]]),
            ).then_inc(stgs[b], 16)
            scalar.dma_start(
                bass.AP(stage_d, b * BPITCH + P_SUB, [[WROW, 64], [2 * P_SUB, 2], [1, WROW]]),
                bass.AP(wpad, 64 * 2 * WROW, [[2 * WROW, 64], [WROW, 2], [1, WROW]]),
            ).then_inc(stgs[b], 16)

        nextp = 0
        for i in range(4, NT):
            if i == 5:
                scalar.wait_ge(sems["ln1s"], 2)
                scalar.activation(scr1[:, 9:17], scv1[:, 9:17], ACT.Ln, bias=cst[:, 0:1])
                scalar.drain()
                scalar.activation(scr1[:, 9:17], scr1[:, 9:17], ACT.Exp, scale=cst[:, 1:2])
                scalar.drain().then_inc(sems["rinv1"], 1)
            if i == 8:
                scalar.wait_ge(sems["cols1"], 2)
            scalar.activation(xn1[:, i * C:(i + 1) * C], xs_t(i), ACT.Identity,
                              bias=scn1[:, i + 1:i + 2], scale=scr1[:, i + 1:i + 2]).then_inc(sems["xn1a"], 1)
            if level >= 2 and i % 4 == 3:
                w = i // 4
                scalar.drain()
                scalar.dma_start(
                    bass.AP(xn1_d, (32 + 512 * w) * C, [[C, 128], [128 * C, 4], [1, C]]),
                    bass.AP(xn1, 4 * w * C, [[NT * C, 128], [C, 4], [1, C]]),
                ).then_inc(x1ds[1 + w], 16)
            if level >= 2:
                j = i - 4
                exp_act(j)
                if level >= 3 and j % 2 == 1:
                    stage_wr(j // 2)
            if level >= 2 and i % 4 == 3:
                w = i // 4
                while nextp < 5 and 128 * nextp + 258 <= 32 + 512 * (w + 1):
                    win_read(nextp)
                    nextp += 1
        if level >= 2:
            exp_act(12)
            exp_act(13)
            if level >= 3:
                stage_wr(6)
            exp_act(14)
            exp_act(15)
            if level >= 3:
                stage_wr(7)
        if level >= 3:
            while nextp < 16:
                win_read(nextp)
                nextp += 1

        if level >= 5:
            scalar.wait_ge(sems["ln2s"], 1)
            scalar.activation(scr2[:, 0:8], scv2[:, 0:8], ACT.Ln, bias=cst[:, 0:1])
            scalar.drain()
            scalar.activation(scr2[:, 0:8], scr2[:, 0:8], ACT.Exp, scale=cst[:, 1:2])
            scalar.drain().then_inc(sems["rinv2"], 1)
            scalar.wait_ge(sems["cols2"], 1)
            for i in range(NT):
                if i == 8:
                    scalar.wait_ge(sems["ln2s"], 2)
                    scalar.activation(scr2[:, 8:16], scv2[:, 8:16], ACT.Ln, bias=cst[:, 0:1])
                    scalar.drain()
                    scalar.activation(scr2[:, 8:16], scr2[:, 8:16], ACT.Exp, scale=cst[:, 1:2])
                    scalar.drain().then_inc(sems["rinv2"], 1)
                    scalar.wait_ge(sems["cols2"], 2)
                if i >= 4:
                    scalar.wait_ge(x2ws[i // 4 - 1], 16)
                scalar.activation(xn2r[:, (i % 4) * C:(i % 4 + 1) * C], xs_t(i), ACT.Identity,
                                  bias=scn2[:, i:i + 1], scale=scr2[:, i:i + 1]).then_inc(sems["xn2a"], 1)
                if i % 4 == 3:
                    w_ = i // 4
                    scalar.drain()
                    scalar.dma_start(
                        bass.AP(xn2_d, 4 * w_ * 128 * C, [[C, 128], [128 * C, 4], [1, C]]),
                        bass.AP(xn2r, 0, [[4 * C, 128], [C, 4], [1, C]]),
                    ).then_inc(x2ws[w_], 16)
            if level >= 6:
                for w in range(8):
                    scalar.wait_ge(sems["outs"], 4 * (w + 1))
                    scalar.dma_start(
                        bass.AP(out_d, 2 * w * 128 * C, [[C, 128], [128 * C, 2], [1, C]]),
                        bass.AP(outst, ((2 * w) % 4) * C, [[4 * C, 128], [C, 2], [1, C]]),
                    ).then_inc(ods[w], 16)


    @block.tensor
    def _(tensor):
        tensor.wait_ge(sems["ldwl"], 16)
        tensor.wait_ge(sems["vprep"], 1)
        tensor.wait_ge(sems["vsq"], 1)
        for cc in range(8):
            mm = tensor.matmul(ps_wp[1][0:1, 0:F], onesbc[:], wet[:, cc * F:(cc + 1) * F],
                               start=(cc == 0), stop=(cc == 7))
        mm.then_inc(sems["mrt"], 1)
        tensor.wait_ge(sems["vabs"], 1)
        tensor.matmul(ps_at[0][0:1, 0:1], onesfc[:], redt[:, 8:9],
                      start=True, stop=True).then_inc(sems["mrt"], 1)
        tensor.wait_ge(sems["scell"], 1)
        tensor.matmul(ps_at[1][:, 0:8], onesfr[:], sc1[0:1, :],
                      start=True, stop=True).then_inc(sems["mrt"], 1)

        wpbanks = [ps_wp[0], ps_wp[1], ps_ff[0], ps_ff[1]]

        def wproj_tile(i):
            if i % 4 == 0:
                tensor.wait_ge(x1ts[i // 4], 128)
            if i >= 4:
                tensor.wait_ge(sems["expd"], i - 3)
            pw = wpbanks[i % 4]
            for cc in range(8):
                mm = tensor.matmul(pw[:, 0:F], xn1T[:, cc * T + i * 128:cc * T + (i + 1) * 128],
                                   wlTb[:, cc * F:(cc + 1) * F], start=(cc == 0), stop=(cc == 7))
            mm.then_inc(sems["wp"], 1)

        def agg_pair(i):
            b = i // 2
            tensor.wait_ge(wetbs[b], 16)
            tensor.wait_ge(winbs[i], 16)
            if i == 0:
                tensor.wait_ge(sems["vcell"], 1)
            if i == 1:
                tensor.wait_ge(sems["bcv"], 1)
            if i >= 2:
                tensor.wait_ge(sems["x1"], i - 1)
            pa = ps_at[i % 2]
            for jj in range(2):
                j = 2 * i + jj
                bb = b % 3
                colbase = (j - 4 * b) * 1025
                s = (2 * i) % 10 + jj
                for h in range(H):
                    if bb == 2:
                        lhsT = bass.AP(wet3, colbase + h, [[WET_ROWS_PAD, 128], [16, 64]])
                    else:
                        lhsT = bass.AP(wet, bb * WET_ROWS_PAD + colbase + h,
                                       [[2 * WET_ROWS_PAD, 128], [16, 64]])
                    rhs = wins[:, s * C + h * R:s * C + (h + 1) * R]
                    mm = tensor.matmul(pa[64 * jj:64 * jj + 64, h * R:(h + 1) * R], lhsT, rhs,
                                       start=True, stop=True)
                mm.then_inc(sems["agg"], 1)

        if level >= 2:
            tensor.wait_ge(sems["vcell"], 1)
            for i in range(NT):
                wproj_tile(i)
            if level >= 4:
                for i in range(NT):
                    agg_pair(i)

        if level >= 6:
            tensor.wait_ge(sems["tern"], 8)
            tensor.wait_ge(sems["ldba"], 32)
            for i in range(NT):
                if i % 4 == 0:
                    tensor.wait_ge(xnts[i // 4], 128)
                if i >= 2:
                    tensor.wait_ge(sems["outs"], 2 * (i - 2) + 2)
                fb = [ps_ff[0], ps_ff[1]] if i % 2 == 0 else [ps_wp[0], ps_wp[1]]
                for cc in range(8):
                    lhs = xn2T[:, cc * T + i * 128:cc * T + (i + 1) * 128]
                    tensor.matmul(fb[0][:], lhs, wtern[:, cc * C:cc * C + 512],
                                  start=(cc == 0), stop=False)
                    tensor.matmul(fb[1][:], lhs, wtern[:, cc * C + 512:cc * C + 1024],
                                  start=(cc == 0), stop=False)
                tensor.matmul(fb[0][:], onesbr[0:1, :], biasb[0:1, 0:512],
                              start=False, stop=True).then_inc(sems["ffn"], 1)
                tensor.matmul(fb[1][:], onesbr[0:1, :], biasb[0:1, 512:1024],
                              start=False, stop=True).then_inc(sems["ffn"], 1)

    stack.close()
    return nc


def make_inputs(x, wl, wf, al_, bi):
    x = np.asarray(x, dtype=np.float32)
    wlT = np.ascontiguousarray(np.asarray(wl, dtype=np.float32).T)
    wT = np.ascontiguousarray(np.asarray(wf, dtype=np.float32).T)
    al_ = np.asarray(al_, dtype=np.float32).reshape(1, 1)
    bi = np.asarray(bi, dtype=np.float32).reshape(1, C)
    in_maps = []
    for c in range(NCORES):
        seq, half = c // 2, c % 2
        xs_ = np.ascontiguousarray(x[seq, half * T:(half + 1) * T])
        if half == 0:
            haloa = np.zeros((32, C), dtype=np.float32)
        else:
            haloa = np.ascontiguousarray(x[seq, half * T - 32:half * T])
        in_maps.append({"x": xs_, "halo": haloa, "wlT": wlT, "wT": wT,
                        "bias": bi, "alpha": al_})
    return in_maps


def kernel(L0_attn_x, L0_attn_w_lin, L0_ffn_weight, L0_ffn_alpha, L0_ffn_bias):
    in_maps = make_inputs(L0_attn_x, L0_attn_w_lin, L0_ffn_weight, L0_ffn_alpha, L0_ffn_bias)
    nc = build_nc(level=LVL_FULL)
    res = run_bass_kernel_spmd(nc, in_maps, core_ids=list(range(NCORES)))
    out = np.empty((4, 4096, C), dtype=np.float32)
    for c in range(NCORES):
        seq, half = c // 2, c % 2
        out[seq, half * T:(half + 1) * T] = res.results[c]["out"]
    return out
